# version 13
# baseline (speedup 1.0000x reference)
import contextlib
import os
import numpy as np
import concourse.bass as bass
import concourse.mybir as mybir
from concourse.bass_utils import run_bass_kernel_spmd

F32 = mybir.dt.float32
BF16 = mybir.dt.bfloat16
AF = mybir.ActivationFunctionType
ALU = mybir.AluOpType
AX = mybir.AxisListType

NCORE = 8
DM = 1024
DIN = 1024
L = 4096
TOK = 2048
NST = 16
DTR = 64
TC = 512
NCH = L // TC
NT = TOK // 512
EPS_LN = 1e-5
EPS_GATE = 1e-6
GRP = [[0, 1], [2, 3], [4, 5], [6, 7]]

ENGINES = ("tensor", "vector", "scalar", "gpsimd", "sync")


class Prog:

    def __init__(self, nc):
        self.nc = nc
        self.ops = {e: [] for e in ENGINES}
        self.cnt = {e: 0 for e in ENGINES}
        self.dma_cnt = {}
        self.dma_streams = []

    def op(self, eng, fn, deps=()):
        idx = self.cnt[eng] + 1
        w = self._waits(deps)
        if eng != "tensor" and idx > 1 and w.get(eng, 0) < idx - 1:
            w[eng] = idx - 1
        self.ops[eng].append(("op", fn, w))
        self.cnt[eng] = idx
        return (eng, idx)

    def dma(self, eng, stream, fn, deps=()):
        if stream not in self.dma_cnt:
            self.dma_cnt[stream] = 0
            self.dma_streams.append(stream)
        self.dma_cnt[stream] += 16
        self.ops[eng].append(("dma", fn, self._waits(deps), stream))
        return ("dma:" + stream, self.dma_cnt[stream])

    def wait(self, eng, deps):
        self.ops[eng].append(("wait", self._waits(deps)))

    def _waits(self, deps):
        acc = {}
        for d in deps:
            if d is None:
                continue
            src, val = d
            if val > acc.get(src, 0):
                acc[src] = val
        return acc

    def barrier(self, skip=()):
        toks = [(e, self.cnt[e]) for e in ENGINES if self.cnt[e] > 0 and e not in skip]
        toks += [("dma:" + s, self.dma_cnt[s]) for s in self.dma_streams]
        for e in ENGINES:
            self.wait(e, [t for t in toks if t[0] != e])

    def emit(self):
        nc = self.nc
        self.psem = {e: nc.alloc_semaphore(f"prog_{e}") for e in ENGINES}
        self.dsem = {s: nc.alloc_semaphore(f"dma_{s}") for s in self.dma_streams}

        def sem_of(src):
            return self.dsem[src[4:]] if src.startswith("dma:") else self.psem[src]

        with nc.Block() as block:
            for eng in ENGINES:
                if not self.ops[eng]:
                    continue

                def body(e, eng=eng):
                    for item in self.ops[eng]:
                        kind = item[0]
                        if kind == "wait":
                            for src, val in item[1].items():
                                e.wait_ge(sem_of(src), val)
                            continue
                        fn, waits = item[1], item[2]
                        witems = list(waits.items())
                        for src, val in witems[1:]:
                            e.wait_ge(sem_of(src), val)
                        inst = fn(e)
                        if witems:
                            inst._wait_ge(sem_of(witems[0][0]), witems[0][1])
                        if kind == "dma":
                            inst.then_inc(self.dsem[item[3]], 16)
                        else:
                            inst.then_inc(self.psem[eng], 1)

                getattr(block, eng)(body)


def build_program():
    nc = bass.Bass(trn_type="TRN2", target_bir_lowering=False, num_devices=NCORE)

    DI = lambda n, s: nc.dram_tensor(n, s, F32, kind="ExternalInput").ap()
    x_in = DI("x_in", [DM, TOK])
    winT = DI("winT", [DM, 2 * DIN])
    convw = DI("convw", [128, 32])
    convb = DI("convb", [128, 8])
    xprojT = DI("xprojT", [DIN, 96])
    dtprojT = DI("dtprojT", [DTR, DIN])
    dtb = DI("dtb", [128, 8])
    alog = DI("alog", [128, 8 * NST])
    dvec = DI("dvec", [128, 8])
    woutT = DI("woutT", [8, 128, 8 * 128])
    ffnWT = DI("ffnWT", [DM, DM])
    ffnb = DI("ffnb", [128, 8])
    ffnRT = DI("ffnRT", [DM, 8])
    ffnA2 = DI("ffnA2", [DM, 128])
    ffnB2 = DI("ffnB2", [128, DM])
    identv = DI("identv", [128, 128])
    cvalsv = DI("cvalsv", [128, 2])
    onescv = DI("onescv", [128, 1])
    onesrv = DI("onesrv", [1, 128])

    out_t = nc.dram_tensor("out", [DM, TOK], F32, kind="ExternalOutput")

    cc1_in = nc.dram_tensor("cc1_in", [NT, DM, 512], F32)
    cc1_out = nc.dram_tensor("cc1_out", [NT, 2 * DM, 512], F32)
    x1own = nc.dram_tensor("x1own", [DM, TOK], F32)
    cc2_in = nc.dram_tensor("cc2_in", [NCH, 96, TC], F32)
    cc2_out = nc.dram_tensor("cc2_out", [NCH, 96, TC], F32)
    cc3_in = [nc.dram_tensor(f"cc3i{q}", [2 * DM, 512], F32) for q in range(4)]
    cc3_out = [nc.dram_tensor(f"cc3o{q}", [DM, 512], F32) for q in range(4)]
    xs_dram = nc.dram_tensor("xs_dram", [8, 128, L], F32)
    z_dram = nc.dram_tensor("z_dram", [8, 128, L], F32)
    g1dram = nc.dram_tensor("g1dram", [8, TOK], F32)
    g2dram = nc.dram_tensor("g2dram", [8, TOK], F32)

    p = Prog(nc)
    SB = lambda n, s, dt=F32: nc.alloc_sbuf_tensor(n, s, dt)

    P0 = nc.alloc_psum_tensor("P0", [128, 512], F32)
    P1 = nc.alloc_psum_tensor("P1", [128, 512], F32)
    P2 = nc.alloc_psum_tensor("P2", [128, 512], F32)
    P3 = nc.alloc_psum_tensor("P3", [128, 512], F32)
    PSs = nc.alloc_psum_tensor("PSs", [128, 512], F32)
    PG = nc.alloc_psum_tensor("PG", [8, 512], F32)
    PT = nc.alloc_psum_tensor("PT", [128, 128], F32)
    PX = nc.alloc_psum_tensor("PX", [96, 512], F32)

    T = lambda fn, deps: p.op("tensor", fn, deps)
    V = lambda fn, deps: p.op("vector", fn, deps)
    S = lambda fn, deps: p.op("scalar", fn, deps)

    ident = SB("ident", [128, 128])
    onesc = SB("onesc", [128, 1])
    onesr = SB("onesr", [1, 128])
    p.dma("sync", "cst", lambda e: e.dma_start(out=ident[:], in_=identv))
    p.dma("sync", "cst", lambda e: e.dma_start(out=onesc[:], in_=onescv))
    p.dma("sync", "cst", lambda e: e.dma_start(out=onesr[:], in_=onesrv))
    cvals = SB("cvals", [128, 2])
    p.dma("sync", "cst", lambda e: e.dma_start(out=cvals[:], in_=cvalsv))
    cstd = ("dma:cst", p.dma_cnt["cst"])

    convw_sb = SB("convw_sb", [128, 32])
    convb_sb = SB("convb_sb", [128, 8])
    dtb_sb = SB("dtb_sb", [128, 8])
    dvec_sb = SB("dvec_sb", [128, 8])
    negA_sb = SB("negA_sb", [128, 8 * NST])
    hstash = SB("hstash", [128, 8 * NST])
    p.dma("sync", "vecs", lambda e: e.dma_start(out=convw_sb[:], in_=convw))
    p.dma("sync", "vecs", lambda e: e.dma_start(out=convb_sb[:], in_=convb))
    p.dma("sync", "vecs", lambda e: e.dma_start(out=dtb_sb[:], in_=dtb))
    p.dma("sync", "vecs", lambda e: e.dma_start(out=dvec_sb[:], in_=dvec))
    p.dma("sync", "vecs", lambda e: e.dma_start(out=negA_sb[:], in_=alog))
    vecd = ("dma:vecs", p.dma_cnt["vecs"])
    nA1 = S(lambda e: e.activation(out=negA_sb[:], in_=negA_sb[:], func=AF.Exp), [vecd])
    negA_t = V(lambda e: e.tensor_scalar(out=negA_sb[:], in0=negA_sb[:], scalar1=-1.0,
                                         scalar2=None, op0=ALU.mult), [nA1])
    hs0 = V(lambda e: e.memset(hstash[:], 0.0), [])

    def emit_ffn(tag, X, xdeps, W, gdram, gstream, assemble):
        sc = W["scratch"]
        sq0 = sc["sq0"]
        sq1, acc1, acc2 = sc["sq1"], sc["acc1"], sc["acc2"]
        s1, s2, m_r = sc["s1"], sc["s2"], sc["m"]
        rstd_r = s1
        mr_row = s2
        rstd_rep, mr_rep = sc["rstd_rep"], sc["mr_rep"]
        nWcols = sc["nWcols"]
        nWrow, nRrow, nArow = sc["nWrow"], sc["nRrow"], sc["nArow"]
        glog, GT, grep = sc["glog"], sc["GT"], sc["grep"]
        gfm = glog
        g = sc["gate"]
        h_sb = sc["h_sb"]
        gh = h_sb
        fR, fA = W["RT"], W["A2"]
        wdep = W["wdep"]
        wf = W["wf"]
        Wh, Wl = W["Wh"], W["Wl"]
        Ah, Al, Bh, Bl = W["Ah"], W["Al"], W["Bh"], W["Bl"]
        Xh, Xl = sc["Xh"], sc["Xl"]
        ghh, ghl = sc["ghh"], sc["ghl"]

        rs_mm = [None, None]
        wspl = []
        rs_last = []
        for d in range(8):
            i = d % 2
            wld = p.dma("sync", tag + "wf0", lambda e, d=d, i=i: e.dma_start(
                out=wf[i][:], in_=ffnWT[128 * d:128 * (d + 1), :]),
                (wspl[-1:] + rs_last[-1:]) if wspl else [])
            for c in range(2):
                rs_mm[c] = T(lambda e, d=d, c=c, i=i: e.matmul(
                    PSs[32 * c:32 * c + 1, :], onesc[:], wf[i][:, 512 * c:512 * (c + 1)],
                    start=(d == 0), stop=(d == 7)), [wld, cstd])
            rs_last.append(rs_mm[1])
            sh = S(lambda e, d=d, i=i: e.copy(out=Wh[d][:], in_=wf[i][:]), [wld])
            wspl.append(V(lambda e, d=d, i=i: e.tensor_tensor(
                out=Wl[d][:], in0=wf[i][:], in1=Wh[d][:], op=ALU.subtract), [sh]))
        nwev = [S(lambda e, c=c: e.copy(out=nWrow[0:1, 512 * c:512 * (c + 1)],
                                        in_=PSs[32 * c:32 * c + 1, :]), [rs_mm[c]])
                for c in range(2)]
        nW_t = V(lambda e: e.tensor_scalar(out=nWrow[:], in0=nWrow[:], scalar1=-1.0,
                                           scalar2=None, op0=ALU.mult), nwev)

        ev1s, ev2s = [], []
        mm1d, mm2d = None, None
        for t in range(NT):
            av = V(lambda e, t=t: e.tensor_tensor(
                out=acc1[:], in0=X[0][:, 512 * t:512 * (t + 1)],
                in1=X[1][:, 512 * t:512 * (t + 1)], op=ALU.add),
                [xdeps[0], xdeps[1]] + ([mm1d] if mm1d else []))
            for d in range(2, 8):
                av = V(lambda e, d=d, t=t: e.tensor_tensor(
                    out=acc1[:], in0=acc1[:], in1=X[d][:, 512 * t:512 * (t + 1)],
                    op=ALU.add), [xdeps[d], av])
            mm1d = T(lambda e: e.matmul(PSs[0:1, :], onesc[:], acc1[:],
                                        start=True, stop=True),
                     [av, cstd] + ev1s[-1:] + ev2s[-1:] + (nwev if t == 0 else []))
            ev1s.append(S(lambda e, t=t: e.copy(out=s1[0:1, 512 * t:512 * (t + 1)],
                                                in_=PSs[0:1, :]), [mm1d]))
        sqp = {0: None, 1: None}
        for t in range(NT):
            q0 = S(lambda e, t=t: e.square(out=sq0[:], in_=X[0][:, 512 * t:512 * (t + 1)]),
                   [xdeps[0]] + ([sqp[0]] if sqp[0] else []))
            q1_ = S(lambda e, t=t: e.square(out=sq1[:], in_=X[1][:, 512 * t:512 * (t + 1)]),
                    [xdeps[1]] + ([sqp[1]] if sqp[1] else []))
            av = V(lambda e: e.tensor_tensor(out=acc2[:], in0=sq0[:], in1=sq1[:],
                                             op=ALU.add), [q0, q1_] + ([mm2d] if mm2d else []))
            sqp[0], sqp[1] = av, av
            for d in range(2, 8):
                qd = S(lambda e, d=d, t=t: e.square(
                    out=(sq0 if d % 2 == 0 else sq1)[:], in_=X[d][:, 512 * t:512 * (t + 1)]),
                    [xdeps[d], sqp[d % 2]])
                av = V(lambda e, d=d: e.tensor_tensor(
                    out=acc2[:], in0=acc2[:], in1=(sq0 if d % 2 == 0 else sq1)[:],
                    op=ALU.add), [qd, av])
                sqp[d % 2] = av
            mm2d = T(lambda e: e.matmul(PSs[32:33, :], onesc[:], acc2[:],
                                        start=True, stop=True),
                     [av, cstd, ev1s[-1]] + ev2s[-1:] + (nwev if t == 0 else []))
            ev2s.append(S(lambda e, t=t: e.copy(out=s2[0:1, 512 * t:512 * (t + 1)],
                                                in_=PSs[32:33, :]), [mm2d]))
        r1 = V(lambda e: e.tensor_scalar(out=m_r[:], in0=s1[:], scalar1=1.0 / DM,
                                         scalar2=None, op0=ALU.mult), ev1s)
        r2 = V(lambda e: e.tensor_scalar(out=s2[:], in0=s2[:], scalar1=1.0 / DM,
                                         scalar2=None, op0=ALU.mult), ev2s)
        r3 = V(lambda e: e.tensor_tensor(out=s1[:], in0=m_r[:], in1=m_r[:], op=ALU.mult), [r1])
        r4 = V(lambda e: e.tensor_tensor(out=s2[:], in0=s2[:], in1=s1[:],
                                         op=ALU.subtract), [r2, r3])
        r5 = S(lambda e: e.activation(out=s2[:], in_=s2[:], func=AF.Ln,
                                      bias=cvals[0:1, 0:1]), [r4, cstd])
        r6 = S(lambda e: e.activation(out=rstd_r[:], in_=s2[:], func=AF.Exp,
                                      scale=cvals[0:1, 1:2]), [r5])

        def rowsum(dst, tiles, width, deps, prev):
            evs = list(prev)
            for c in range((width + 511) // 512):
                w = min(512, width - 512 * c)
                mm = None
                for d in range(8):
                    mm = T(lambda e, d=d, c=c, w=w: e.matmul(
                        PSs[64:65, 0:w], onesc[:], tiles[d][:, 512 * c:512 * c + w],
                        start=(d == 0), stop=(d == 7)), deps + [cstd] + evs[-1:])
                evs.append(S(lambda e, c=c, w=w: e.copy(
                    out=dst[0:1, 512 * c:512 * c + w], in_=PSs[64:65, 0:w]), [mm]))
            neg = V(lambda e: e.tensor_scalar(out=dst[:], in0=dst[:], scalar1=-1.0,
                                              scalar2=None, op0=ALU.mult), evs[-1:])
            return neg, evs[-1]

        nR_t, e_b = rowsum(nRrow, fR, 8, [wdep], ev2s[-1:] + ev1s[-1:])
        nA_t, e_c = rowsum(nArow, fA, 128, [wdep], [e_b])

        trs_c = []
        for c8 in range(8):
            trs_c.append(T(lambda e, c8=c8: e.matmul(
                PT[:, c8:c8 + 1], nWrow[0:1, 128 * c8:128 * (c8 + 1)], ident[0:1, 0:1],
                is_transpose=True, start=True, stop=True, skip_group_check=True),
                [nW_t, cstd]))
        trs_c.append(T(lambda e: e.matmul(
            PT[0:8, 8:9], nRrow[0:1, 0:8], ident[0:1, 0:1],
            is_transpose=True, start=True, stop=True, skip_group_check=True), [nR_t, cstd]))
        trs_c.append(T(lambda e: e.matmul(
            PT[:, 9:10], nArow[0:1, :], ident[0:1, 0:1],
            is_transpose=True, start=True, stop=True, skip_group_check=True), [nA_t, cstd]))
        S(lambda e: e.copy(out=nWcols[:, 0:8], in_=PT[:, 0:8]), trs_c)
        S(lambda e: e.copy(out=nWcols[0:8, 8:9], in_=PT[0:8, 8:9]), [])
        trc0 = S(lambda e: e.copy(out=nWcols[:, 9:10], in_=PT[:, 9:10]), [])
        trc = V(lambda e: e.tensor_scalar(out=nWcols[:, 10:18], in0=nWcols[:, 0:8],
                                          scalar1=0.5, scalar2=None, op0=ALU.mult), [trc0])

        mr_v = V(lambda e: e.tensor_tensor(out=mr_row[:], in0=m_r[:], in1=rstd_r[:],
                                           op=ALU.mult), [r6, r1])
        repev = []
        for t in range(NT):
            mmb = T(lambda e, t=t: e.matmul(PSs[:], onesr[:], rstd_r[0:1, 512 * t:512 * (t + 1)],
                                            start=True, stop=True),
                    [r6, cstd, e_c] + repev[-1:])
            repev.append(S(lambda e, t=t: e.copy(out=rstd_rep[:, 512 * t:512 * (t + 1)],
                                                 in_=PSs[:]), [mmb]))
            mmb2 = T(lambda e, t=t: e.matmul(PSs[:], onesr[:], mr_row[0:1, 512 * t:512 * (t + 1)],
                                             start=True, stop=True),
                     [mr_v, cstd] + repev[-1:])
            repev.append(S(lambda e, t=t: e.copy(out=mr_rep[:, 512 * t:512 * (t + 1)],
                                                 in_=PSs[:]), [mmb2]))
        repd = repev[-1]

        lg_ev = []
        for t in range(NT):
            mm = None
            for d in range(8):
                mm = T(lambda e, d=d, t=t: e.matmul(
                    PG[0:8, :], fR[d][:], X[d][:, 512 * t:512 * (t + 1)],
                    start=(d == 0), stop=(d == 7)), [xdeps[d], wdep] + lg_ev[-1:])
            lgm = V(lambda e, t=t: e.tensor_tensor(
                out=glog[:, 512 * t:512 * (t + 1)], in0=PG[0:8, :],
                in1=rstd_rep[0:8, 512 * t:512 * (t + 1)], op=ALU.mult), [mm, repd])
            lg_ev.append(V(lambda e, t=t: e.scalar_tensor_tensor(
                out=glog[:, 512 * t:512 * (t + 1)], in0=mr_rep[0:8, 512 * t:512 * (t + 1)],
                scalar=nWcols[0:8, 8:9], in1=glog[:, 512 * t:512 * (t + 1)],
                op0=ALU.mult, op1=ALU.add), [lgm, trc]))

        trs = []
        for j in range(16):
            trs.append(T(lambda e, j=j: e.matmul(
                PT[:, 8 * j:8 * (j + 1)], glog[0:8, 128 * j:128 * (j + 1)], ident[0:8, 0:8],
                is_transpose=True, start=True, stop=True, skip_group_check=True),
                [lg_ev[j // 4], cstd, trc0]))
        gt_ev = S(lambda e: e.copy(out=GT[:], in_=PT[:]), trs + lg_ev)

        GT3 = lambda t_: t_[:].rearrange("p (a b) -> p a b", b=8)
        g1 = V(lambda e: e.reduce_max(out=g["mx"][:], in_=GT3(GT), axis=AX.X), [gt_ev])
        g2 = V(lambda e: e.tensor_tensor(out=GT3(g["t0"]), in0=GT3(GT),
                                         in1=g["mx"][:].unsqueeze(2).broadcast_to((128, 16, 8)),
                                         op=ALU.subtract), [g1])
        g3 = S(lambda e: e.activation(out=g["t1"][:], in_=g["t0"][:], func=AF.Exp), [g2])
        g4 = V(lambda e: e.reduce_sum(out=g["sm"][:], in_=GT3(g["t1"]), axis=AX.X), [g3])
        g5 = V(lambda e: e.reciprocal(out=g["rc"][:], in_=g["sm"][:]), [g4])
        g6 = V(lambda e: e.tensor_tensor(out=GT3(g["t0"]), in0=GT3(g["t1"]),
                                         in1=g["rc"][:].unsqueeze(2).broadcast_to((128, 16, 8)),
                                         op=ALU.mult), [g5, g3])
        topd = []
        for j in range(16):
            topd.append(V(lambda e, j=j: e.max(out=g["tp"][:, 8 * j:8 * (j + 1)],
                                               in_=g["t0"][:, 8 * j:8 * (j + 1)]), [g6]))
        g7 = V(lambda e: e.tensor_tensor(
            out=GT3(g["t1"]), in0=GT3(g["t0"]),
            in1=g["tp"][:].rearrange("p (a b) -> p a b", b=8)[:, :, 1:2].broadcast_to((128, 16, 8)),
            op=ALU.is_ge), topd)
        g8 = V(lambda e: e.tensor_tensor(out=g["t1"][:], in0=g["t0"][:], in1=g["t1"][:],
                                         op=ALU.mult), [g7])
        g9 = V(lambda e: e.reduce_sum(out=g["sm"][:], in_=GT3(g["t1"]), axis=AX.X), [g8])
        g10 = V(lambda e: e.tensor_scalar(out=g["sm"][:], in0=g["sm"][:], scalar1=EPS_GATE,
                                          scalar2=None, op0=ALU.add), [g9])
        g11 = V(lambda e: e.reciprocal(out=g["rc"][:], in_=g["sm"][:]), [g10])
        g12 = V(lambda e: e.tensor_tensor(out=GT3(g["t0"]), in0=GT3(g["t1"]),
                                          in1=g["rc"][:].unsqueeze(2).broadcast_to((128, 16, 8)),
                                          op=ALU.mult), [g11, g8])

        gfm_ev = []
        for t in range(NT):
            trb = []
            for jj in range(4):
                j = 4 * t + jj
                trb.append(T(lambda e, j=j, jj=jj: e.matmul(
                    PG[0:8, 128 * jj:128 * (jj + 1)], g["t0"][:, 8 * j:8 * (j + 1)], ident[:],
                    is_transpose=True, start=True, stop=True, skip_group_check=True),
                    [g12, cstd] + gfm_ev[-1:] + lg_ev[-1:]))
            gfm_ev.append(S(lambda e, t=t: e.copy(out=gfm[0:8, 512 * t:512 * (t + 1)],
                                                  in_=PG[0:8, :]), trb))
        up = p.dma("gpsimd", tag + "gup", lambda e: e.dma_start(out=gdram.ap(), in_=gfm[0:8, :]),
                   gfm_ev)
        repds = []
        for r in range(16):
            repds.append(p.dma("gpsimd", tag + "grep", lambda e, r=r: e.dma_start(
                out=grep[8 * r:8 * (r + 1), :], in_=gdram.ap()), [up]))

        freeB = {0: None, 1: None}
        freeL = None
        tok = {}
        for t in range(NT):
            xspl = []
            for d in range(8):
                shx = S(lambda e, d=d, t=t: e.copy(out=Xh[d][:], in_=X[d][:, 512 * t:512 * (t + 1)]),
                        [xdeps[d]] + ([tok[(t - 1, "xfree")]] if (t - 1, "xfree") in tok else []))
                xspl.append(V(lambda e, d=d, t=t: e.tensor_tensor(
                    out=Xl[d][:], in0=X[d][:, 512 * t:512 * (t + 1)], in1=Xh[d][:],
                    op=ALU.subtract), [shx]))
            mm = None
            for d in range(8):
                deph = [W["adep"], xspl[d]] + ([tok[(t - 1, "hfree")]] if (d == 0 and (t - 1, "hfree") in tok) else [])
                mm = T(lambda e, d=d: e.matmul(P3[:], Ah[d][:], Xh[d][:],
                                               start=(d == 0), stop=False), deph)
                mm = T(lambda e, d=d: e.matmul(P3[:], Ah[d][:], Xl[d][:],
                                               start=False, stop=False), [])
                mm = T(lambda e, d=d: e.matmul(P3[:], Al[d][:], Xh[d][:],
                                               start=False, stop=(d == 7)), [])
            hv0 = V(lambda e, t=t: e.tensor_tensor(
                out=h_sb[:, 512 * t:512 * (t + 1)], in0=P3[:],
                in1=rstd_rep[:, 512 * t:512 * (t + 1)], op=ALU.mult), [mm, repd])
            hv = V(lambda e, t=t: e.scalar_tensor_tensor(
                out=h_sb[:, 512 * t:512 * (t + 1)], in0=mr_rep[:, 512 * t:512 * (t + 1)],
                scalar=nWcols[:, 9:10], in1=h_sb[:, 512 * t:512 * (t + 1)],
                op0=ALU.mult, op1=ALU.add), [hv0, trc])
            tok[(t, "hfree")] = hv
            ghv = V(lambda e, t=t: e.tensor_tensor(
                out=gh[:, 512 * t:512 * (t + 1)], in0=h_sb[:, 512 * t:512 * (t + 1)],
                in1=grep[:, 512 * t:512 * (t + 1)], op=ALU.mult), [hv] + repds)
            shg = S(lambda e, t=t: e.copy(out=ghh[:], in_=gh[:, 512 * t:512 * (t + 1)]),
                    [ghv] + ([tok[(t - 1, "gfree")]] if (t - 1, "gfree") in tok else []))
            svg = V(lambda e, t=t: e.tensor_tensor(
                out=ghl[:], in0=gh[:, 512 * t:512 * (t + 1)], in1=ghh[:],
                op=ALU.subtract), [shg])
            for och in range(8):
                par = (t * 8 + och) % 2
                pb = P0 if par == 0 else P1
                mmb = None
                for d in range(8):
                    dep0 = []
                    if och == 0:
                        dep0 = [wspl[d], xspl[d]]
                    if d == 0 and freeB[par]:
                        dep0 = dep0 + [freeB[par]]
                    mmb = T(lambda e, d=d, och=och, pb=pb: e.matmul(
                        pb[:], Wh[d][:, 128 * och:128 * (och + 1)], Xh[d][:],
                        start=(d == 0), stop=False), dep0)
                    mmb = T(lambda e, d=d, och=och, pb=pb: e.matmul(
                        pb[:], Wh[d][:, 128 * och:128 * (och + 1)], Xl[d][:],
                        start=False, stop=False), [])
                    mmb = T(lambda e, d=d, och=och, pb=pb: e.matmul(
                        pb[:], Wl[d][:, 128 * och:128 * (och + 1)], Xh[d][:],
                        start=False, stop=(d == 7)), [])
                mml = T(lambda e, och=och: e.matmul(
                    P2[:], Bh[:, 128 * och:128 * (och + 1)], ghh[:],
                    start=True, stop=False), [W["bdep2"], svg] + ([freeL] if freeL else []))
                mml = T(lambda e, och=och: e.matmul(
                    P2[:], Bh[:, 128 * och:128 * (och + 1)], ghl[:],
                    start=False, stop=False), [])
                mml = T(lambda e, och=och: e.matmul(
                    P2[:], Bl[:, 128 * och:128 * (och + 1)], ghh[:],
                    start=False, stop=True), [])
                fb_, fl_ = assemble(och, t, pb, mmb, P2, mml,
                                    {"rstd_rep": rstd_rep, "mr_rep": mr_rep,
                                     "nWcols": nWcols, "trc": trc, "repd": repd,
                                     "sq0": sq0, "sqfree": ev2s[-1]})
                freeB[par] = fb_
                freeL = fl_
            tok[(t, "xfree")] = freeB[(t * 8 + 7) % 2]
            tok[(t, "gfree")] = freeL

    def load_ffn_weights(stack, tag):
        SBx = lambda n, s, dt=F32: stack.enter_context(nc.sbuf_tensor(tag + n, s, dt))
        fR = [SBx(f"R{d}", [128, 8]) for d in range(8)]
        fA = [SBx(f"A{d}", [128, 128]) for d in range(8)]
        fB2 = SBx("B2", [128, DM])
        fb = SBx("b", [128, 8])
        wf = [SBx("wf0", [128, DM])] * 2
        Wh = [SBx(f"Wh{d}", [128, DM], BF16) for d in range(8)]
        Wl = [SBx(f"Wl{d}", [128, DM], BF16) for d in range(8)]
        Ah = [SBx(f"Ah{d}", [128, 128], BF16) for d in range(8)]
        Al = [SBx(f"Al{d}", [128, 128], BF16) for d in range(8)]
        Bh = SBx("Bh", [128, DM], BF16)
        Bl = SBx("Bl", [128, DM], BF16)
        st = tag + "w"
        for d in range(8):
            p.dma("sync", st, lambda e, d=d, fR=fR: e.dma_start(out=fR[d][:], in_=ffnRT[128 * d:128 * (d + 1), :]))
            p.dma("sync", st, lambda e, d=d, fA=fA: e.dma_start(out=fA[d][:], in_=ffnA2[128 * d:128 * (d + 1), :]))
        p.dma("sync", st, lambda e, fB2=fB2: e.dma_start(out=fB2[:], in_=ffnB2))
        p.dma("sync", st, lambda e, fb=fb: e.dma_start(out=fb[:], in_=ffnb))
        wdep = ("dma:" + st, p.dma_cnt[st])
        aspl = []
        for d in range(8):
            sh = S(lambda e, d=d, fA=fA, Ah=Ah: e.copy(out=Ah[d][:], in_=fA[d][:]), [wdep])
            aspl.append(V(lambda e, d=d, fA=fA, Ah=Ah, Al=Al: e.tensor_tensor(
                out=Al[d][:], in0=fA[d][:], in1=Ah[d][:], op=ALU.subtract), [sh]))
        shb = S(lambda e, fB2=fB2, Bh=Bh: e.copy(out=Bh[:], in_=fB2[:]), [wdep])
        svb = V(lambda e, fB2=fB2, Bh=Bh, Bl=Bl: e.tensor_tensor(
            out=Bl[:], in0=fB2[:], in1=Bh[:], op=ALU.subtract), [shb])
        halfb = SBx("hb", [128, 8])
        hb_t = V(lambda e, halfb=halfb, fb=fb: e.tensor_scalar(
            out=halfb[:], in0=fb[:], scalar1=0.5, scalar2=None, op0=ALU.mult), [wdep])
        return {"RT": fR, "A2": fA, "wdep": wdep, "halfb": halfb, "hb_t": hb_t,
                "wf": wf, "Wh": Wh, "Wl": Wl, "Ah": Ah, "Al": Al, "Bh": Bh, "Bl": Bl,
                "adep": aspl[-1], "bdep2": svb}

    def mk_scratch(stack, tag):
        E = lambda n, s: stack.enter_context(nc.sbuf_tensor(tag + n, s, F32))
        Eb = lambda n, s: stack.enter_context(nc.sbuf_tensor(tag + n, s, BF16))
        h_sb = E("hsb", [128, TOK])
        mr_rep = E("mrrep", [128, TOK])
        return {
            "sq0": E("sq0", [128, 512]), "sq1": h_sb[:, 1024:1536],
            "acc1": h_sb[:, 0:512], "acc2": h_sb[:, 512:1024],
            "s1": E("s1", [1, TOK]), "s2": E("s2", [1, TOK]), "m": mr_rep[0:1, :],
            "rstd_rep": E("rrep", [128, TOK]), "mr_rep": mr_rep,
            "nWcols": E("nWcols", [128, 18]),
            "nWrow": E("nW", [1, DM]), "nRrow": E("nR", [1, 8]), "nArow": E("nA", [1, 128]),
            "glog": E("glog", [8, TOK]), "GT": E("GT", [128, 128]),
            "gate": {"mx": E("gmx", [128, 16]), "sm": E("gsm", [128, 16]),
                     "rc": E("grc", [128, 16]), "tp": E("gtp", [128, 128]),
                     "t0": E("gt0", [128, 128]), "t1": E("gt1", [128, 128])},
            "grep": E("grep2", [128, TOK]),
            "h_sb": h_sb,
            "Xh": [Eb(f"Xh{d}", [128, 512]) for d in range(8)],
            "Xl": [Eb(f"Xl{d}", [128, 512]) for d in range(8)],
            "ghh": Eb("ghh", [128, 512]),
            "ghl": Eb("ghl", [128, 512]),
        }

    with contextlib.ExitStack() as stA:
        X0 = [stA.enter_context(nc.sbuf_tensor(f"X0_{d}", [128, TOK], F32)) for d in range(8)]
        x0d = [p.dma("sync", f"x0_{d}", lambda e, d=d: e.dma_start(
            out=X0[d][:], in_=x_in[128 * d:128 * (d + 1), :])) for d in range(8)]
        Wd = load_ffn_weights(stA, "fa")
        Wd["scratch"] = mk_scratch(stA, "fa")
        x1tile = stA.enter_context(nc.sbuf_tensor("x1tile", [128, 512], F32))
        t1buf = stA.enter_context(nc.sbuf_tensor("t1buf", [128, 512], F32))

        up1 = []
        upt = []
        ag4 = [None] * NT
        prevA = {}

        def asmA(och, t, pb, mmb, pl, mml, aux):
            par = (t * 8 + och) % 2
            tb = t1buf if par == 0 else aux["sq0"]
            a1 = V(lambda e, pb=pb, t=t, tb=tb: e.scalar_tensor_tensor(
                out=tb[:], in0=pb[:], scalar=0.5,
                in1=Wd["scratch"]["rstd_rep"][:, 512 * t:512 * (t + 1)],
                op0=ALU.mult, op1=ALU.mult),
                [mmb, aux["repd"], aux["sqfree"]] +
                ([prevA.get(("a3", par))] if prevA.get(("a3", par)) else []))
            a1 = V(lambda e, och=och, t=t, tb=tb: e.scalar_tensor_tensor(
                out=tb[:], in0=aux["mr_rep"][:, 512 * t:512 * (t + 1)],
                scalar=aux["nWcols"][:, 10 + och:11 + och], in1=tb[:],
                op0=ALU.mult, op1=ALU.add), [a1, aux["trc"]])
            a2 = V(lambda e, pl=pl, och=och, t=t: e.scalar_tensor_tensor(
                out=x1tile[:], in0=pl[:], scalar=0.5, in1=X0[och][:, 512 * t:512 * (t + 1)],
                op0=ALU.mult, op1=ALU.add),
                [mml] + ([prevA.get("u1"), prevA.get("u2")] if prevA.get("u1") else []))
            a3 = V(lambda e, och=och, tb=tb: e.scalar_tensor_tensor(
                out=x1tile[:], in0=tb[:], scalar=Wd["halfb"][:, och:och + 1], in1=x1tile[:],
                op0=ALU.add, op1=ALU.add), [a1, a2, Wd["hb_t"]])
            u1 = p.dma("sync", "x1up", lambda e, och=och, t=t: e.dma_start(
                out=cc1_in.ap()[t][128 * och:128 * (och + 1), :],
                in_=x1tile[:]), [a3])
            u2 = p.dma("scalar", "x1up2", lambda e, och=och, t=t: e.dma_start(
                out=x1own.ap()[128 * och:128 * (och + 1), 512 * t:512 * (t + 1)],
                in_=x1tile[:]), [a3])
            up1.extend([u1, u2])
            upt.append(u1)
            if och == 7:
                ag4[t] = p.op("gpsimd", lambda e, t=t: e.collective_compute(
                    "AllGather", ALU.bypass, replica_groups=GRP,
                    ins=[cc1_in.ap()[t].opt()], outs=[cc1_out.ap()[t].opt()]), list(upt))
                del upt[:]
            prevA[("a3", par)] = a3
            prevA["u1"], prevA["u2"] = u1, u2
            return a1, a2

        emit_ffn("fa", X0, x0d, Wd, g1dram, "fa", asmA)
        if os.environ.get("KBISECT") == "A":
            st = p.dma("sync", "outst", lambda e: e.dma_start(out=out_t.ap()[0:128, 0:512],
                                                              in_=x1tile[:]), [prevA["a3"]])
            p.wait("sync", up1 + [st])
            p.emit()
            return nc
    p.barrier(skip=("gpsimd",))

    with contextlib.ExitStack() as stF:
        E = lambda n, s: stF.enter_context(nc.sbuf_tensor(n, s, F32))
        wf = [E(f"wf{i}", [128, 2 * DIN]) for i in range(2)]
        winh = [stF.enter_context(nc.sbuf_tensor(f"winh{d}", [128, 2 * DIN], BF16)) for d in range(8)]
        winl = [stF.enter_context(nc.sbuf_tensor(f"winl{d}", [128, 2 * DIN], BF16)) for d in range(8)]
        xp = [E(f"xp{d}", [128, 96]) for d in range(8)]
        for d in range(8):
            p.dma("sync", "xpw", lambda e, d=d: e.dma_start(
                out=xp[d][:], in_=xprojT[128 * d:128 * (d + 1), :]))
        xpdep = ("dma:xpw", p.dma_cnt["xpw"])

        nWin = E("nWin", [1, 2 * DIN])
        nWin_col = E("nWin_col", [128, 16])
        x1c = [E(f"x1c_{d}", [128, TC]) for d in range(8)]
        x1h = [[stF.enter_context(nc.sbuf_tensor(f"x1h{b}_{d}", [128, TC], BF16)) for d in range(8)] for b in range(2)]
        x1l = [[stF.enter_context(nc.sbuf_tensor(f"x1l{b}_{d}", [128, TC], BF16)) for d in range(8)] for b in range(2)]
        sqf = E("sqf", [128, TC])
        accf1 = wf[0][:, 0:512]
        accf2 = wf[0][:, 512:1024]
        sqf1 = wf[0][:, 1024:1536]
        mr2r = wf[1][0:1, 0:1024]
        mr2rep = [wf[1][:, 1024:1536], wf[1][:, 1536:2048]]
        xdblb = E("xdblb", [96, TC])
        m2r = E("m2r", [1, 2 * TC])
        v2r = E("v2r", [1, 2 * TC])
        rstd2r = E("rstd2r", [1, 2 * TC])
        r2rep = [E(f"r2rep{b}", [128, TC]) for b in range(2)]
        xcb = [[E(f"xcb{b}_{d}", [128, TC + 3]) for d in range(8)] for b in range(2)]
        xsb = [E(f"xsb{d}", [128, TC]) for d in range(4)]
        zbf = E("zbf", [128, TC])

        rs_mm = [None] * 4
        spl_evs = []
        rs_last = []
        for d in range(8):
            i = d % 2
            wld = p.dma("sync", f"winw{i}", lambda e, d=d, i=i: e.dma_start(
                out=wf[i][:], in_=winT[128 * d:128 * (d + 1), :]),
                (spl_evs[-2:-1] + rs_last[-2:-1]) if len(spl_evs) >= 2 else [])
            for c in range(4):
                pr = PSs[32 * c:32 * c + 1, :] if c < 2 else PX[32 * (c - 2):32 * (c - 2) + 1, :]
                rs_mm[c] = T(lambda e, d=d, c=c, i=i, pr=pr: e.matmul(
                    pr, onesc[:], wf[i][:, 512 * c:512 * (c + 1)],
                    start=(d == 0), stop=(d == 7)), [wld, cstd])
            rs_last.append(rs_mm[3])
            sh = S(lambda e, d=d, i=i: e.copy(out=winh[d][:], in_=wf[i][:]), [wld])
            spl_evs.append(V(lambda e, d=d, i=i: e.tensor_tensor(
                out=winl[d][:], in0=wf[i][:], in1=winh[d][:], op=ALU.subtract), [sh]))
        nwin_evs = [S(lambda e, c=c: e.copy(
            out=nWin[0:1, 512 * c:512 * (c + 1)],
            in_=(PSs[32 * c:32 * c + 1, :] if c < 2 else PX[32 * (c - 2):32 * (c - 2) + 1, :])),
            [rs_mm[c]]) for c in range(4)]
        nWin_t = V(lambda e: e.tensor_scalar(out=nWin[:], in0=nWin[:], scalar1=-1.0,
                                             scalar2=None, op0=ALU.mult), nwin_evs)
        trs_f = []
        for c8 in range(16):
            trs_f.append(T(lambda e, c8=c8: e.matmul(
                PT[:, c8:c8 + 1], nWin[0:1, 128 * c8:128 * (c8 + 1)], ident[0:1, 0:1],
                is_transpose=True, start=True, stop=True, skip_group_check=True),
                [nWin_t, cstd]))
        trcf = S(lambda e: e.copy(out=nWin_col[:], in_=PT[:, 0:16]), trs_f)
        winspl = {d: spl_evs[d] for d in range(8)}

        fs = {"st1": nwin_evs[-1], "st2": nwin_evs[-1], "reps": nwin_evs[-1]}
        xdbl_ev = [None] * NCH

        def fhead(k):
            b = k % 2
            ldk = []
            for d in range(8):
                dep = [ag4[k % 4]] + ([fs[("x1free", d)]] if ("x1free", d) in fs else []) + \
                    ([fs["acc1"]] if "acc1" in fs else []) + \
                    ([fs["acc2"]] if "acc2" in fs else [])
                ldk.append(p.dma("sync", f"x1c_{d}", lambda e, d=d, k=k: e.dma_start(
                    out=x1c[d][:],
                    in_=cc1_out.ap()[k % 4][(k // 4) * DM + 128 * d:(k // 4) * DM + 128 * (d + 1),
                                            :]), dep))
            av = V(lambda e: e.tensor_tensor(out=accf1[:], in0=x1c[0][:], in1=x1c[1][:],
                                             op=ALU.add),
                   [ldk[0], ldk[1]] + ([fs["acc1"]] if "acc1" in fs else [rs_last[6]]))
            for d in range(2, 8):
                av = V(lambda e, d=d: e.tensor_tensor(out=accf1[:], in0=accf1[:],
                                                      in1=x1c[d][:], op=ALU.add), [ldk[d], av])
            mm1 = T(lambda e: e.matmul(PSs[0:1, :], onesc[:], accf1[:], start=True, stop=True),
                    [av, cstd, fs["reps"]] + ([fs["st1"]] if "st1" in fs else []))
            fs["acc1"] = mm1
            e1 = S(lambda e, k=k: e.copy(out=m2r[0:1, TC * (k % 2):TC * (k % 2 + 1)], in_=PSs[0:1, :]), [mm1])
            fs["st1"] = e1
            q0s = S(lambda e: e.square(out=sqf[:], in_=x1c[0][:]),
                    [ldk[0]] + ([fs["sqp0"]] if "sqp0" in fs else []))
            q1s = S(lambda e: e.square(out=sqf1[:], in_=x1c[1][:]),
                    [ldk[1]] + ([fs["sqp1"]] if "sqp1" in fs else [rs_last[6]]))
            av = V(lambda e: e.tensor_tensor(out=accf2[:], in0=sqf[:], in1=sqf1[:],
                                             op=ALU.add),
                   [q0s, q1s] + ([fs["acc2"]] if "acc2" in fs else []))
            fs["sqp0"], fs["sqp1"] = av, av
            for d in range(2, 8):
                qd = S(lambda e, d=d: e.square(out=(sqf if d % 2 == 0 else sqf1)[:],
                                               in_=x1c[d][:]),
                       [ldk[d], fs["sqp0" if d % 2 == 0 else "sqp1"]])
                av = V(lambda e, d=d: e.tensor_tensor(
                    out=accf2[:], in0=accf2[:], in1=(sqf if d % 2 == 0 else sqf1)[:],
                    op=ALU.add), [qd, av])
                fs["sqp0" if d % 2 == 0 else "sqp1"] = av
            mm2 = T(lambda e: e.matmul(PSs[32:33, :], onesc[:], accf2[:], start=True, stop=True),
                    [av, cstd, fs["reps"]] + ([fs["st2"]] if "st2" in fs else []))
            fs["acc2"] = mm2
            e2 = S(lambda e, k=k: e.copy(out=v2r[0:1, TC * (k % 2):TC * (k % 2 + 1)], in_=PSs[32:33, :]), [mm2])
            fs["st2"] = e2
            xspl = []
            for d in range(8):
                sh = S(lambda e, d=d, b=b: e.copy(out=x1h[b][d][:], in_=x1c[d][:]),
                       [ldk[d]] + ([fs[("xspl", b, d)]] if ("xspl", b, d) in fs else []))
                sv = V(lambda e, d=d, b=b: e.tensor_tensor(
                    out=x1l[b][d][:], in0=x1c[d][:], in1=x1h[b][d][:], op=ALU.subtract), [sh])
                fs[("xspl", b, d)] = sv
                fs[("x1free", d)] = sv
                xspl.append(sv)
            kb = k % 2
            q1 = V(lambda e, kb=kb: e.tensor_scalar(
                out=m2r[0:1, TC * kb:TC * (kb + 1)], in0=m2r[0:1, TC * kb:TC * (kb + 1)],
                scalar1=1.0 / DM, scalar2=None, op0=ALU.mult), [e1])
            q2 = V(lambda e, kb=kb: e.tensor_scalar(
                out=v2r[0:1, TC * kb:TC * (kb + 1)], in0=v2r[0:1, TC * kb:TC * (kb + 1)],
                scalar1=1.0 / DM, scalar2=None, op0=ALU.mult), [e2])
            q3 = V(lambda e, kb=kb: e.tensor_tensor(
                out=rstd2r[0:1, TC * kb:TC * (kb + 1)], in0=m2r[0:1, TC * kb:TC * (kb + 1)],
                in1=m2r[0:1, TC * kb:TC * (kb + 1)], op=ALU.mult), [q1])
            q4 = V(lambda e, kb=kb: e.tensor_tensor(
                out=v2r[0:1, TC * kb:TC * (kb + 1)], in0=v2r[0:1, TC * kb:TC * (kb + 1)],
                in1=rstd2r[0:1, TC * kb:TC * (kb + 1)], op=ALU.subtract), [q2, q3])
            q5 = S(lambda e, kb=kb: e.activation(out=rstd2r[0:1, TC * kb:TC * (kb + 1)],
                                               in_=v2r[0:1, TC * kb:TC * (kb + 1)],
                                               func=AF.Ln, bias=cvals[0:1, 0:1]), [q4, cstd])
            q6 = S(lambda e, kb=kb: e.activation(out=rstd2r[0:1, TC * kb:TC * (kb + 1)],
                                               in_=rstd2r[0:1, TC * kb:TC * (kb + 1)],
                                               func=AF.Exp, scale=cvals[0:1, 1:2]), [q5])
            mmb = T(lambda e, kb=kb: e.matmul(PSs[:], onesr[:], rstd2r[0:1, TC * kb:TC * (kb + 1)],
                                            start=True, stop=True),
                    [q6, cstd] + ([fs["reps"]] if "reps" in fs else []))
            repev = S(lambda e, b=b: e.copy(out=r2rep[b][:], in_=PSs[:]), [mmb])
            mr2v = V(lambda e, kb=kb: e.tensor_tensor(
                out=mr2r[0:1, TC * kb:TC * (kb + 1)], in0=m2r[0:1, TC * kb:TC * (kb + 1)],
                in1=rstd2r[0:1, TC * kb:TC * (kb + 1)], op=ALU.mult),
                [q6, q1] + ([rs_last[7]] if k == 0 else []))
            mmb2 = T(lambda e, kb=kb: e.matmul(PSs[:], onesr[:], mr2r[0:1, TC * kb:TC * (kb + 1)],
                                               start=True, stop=True), [mr2v, cstd, repev])
            repev2 = S(lambda e, b=b: e.copy(out=mr2rep[b][:], in_=PSs[:]), [mmb2])
            fs["reps"] = repev2
            fs[("hk", k)] = (xspl, repev, repev2)

        def fbody(k):
            b = k % 2
            xspl, repev, repev2 = fs[("hk", k)]
            och_evs = []
            for och in range(16):
                par = och % 2
                pb = P0 if par == 0 else P1
                mm = None
                for d in range(8):
                    dep0 = ([winspl[d], xspl[d]] +
                            ([fs[("ipfree", par)]] if ("ipfree", par) in fs else [])) \
                        if d == 0 else [winspl[d], xspl[d]]
                    mm = T(lambda e, d=d, och=och, b=b, pb=pb: e.matmul(
                        pb[:], winh[d][:, 128 * och:128 * (och + 1)], x1h[b][d][:],
                        start=(d == 0), stop=False), dep0)
                    mm = T(lambda e, d=d, och=och, b=b, pb=pb: e.matmul(
                        pb[:], winh[d][:, 128 * och:128 * (och + 1)], x1l[b][d][:],
                        start=False, stop=False), [])
                    mm = T(lambda e, d=d, och=och, b=b, pb=pb: e.matmul(
                        pb[:], winl[d][:, 128 * och:128 * (och + 1)], x1h[b][d][:],
                        start=False, stop=(d == 7)), [])
                if och < 8:
                    ev = V(lambda e, och=och, b=b, pb=pb: e.tensor_tensor(
                        out=xcb[b][och][:, 3:TC + 3], in0=pb[:], in1=r2rep[b][:], op=ALU.mult),
                        [mm, repev] + ([fs[("xcfree", b, och)]] if ("xcfree", b, och) in fs else []))
                    ev = V(lambda e, och=och, b=b: e.scalar_tensor_tensor(
                        out=xcb[b][och][:, 3:TC + 3], in0=mr2rep[b][:],
                        scalar=nWin_col[:, och:och + 1], in1=xcb[b][och][:, 3:TC + 3],
                        op0=ALU.mult, op1=ALU.add), [ev, repev2, trcf])
                else:
                    ev = V(lambda e, b=b, pb=pb: e.tensor_tensor(
                        out=zbf[:], in0=pb[:], in1=r2rep[b][:], op=ALU.mult),
                        [mm, repev] + ([fs["zup"]] if "zup" in fs else []))
                    ev = V(lambda e, och=och, b=b: e.scalar_tensor_tensor(
                        out=zbf[:], in0=mr2rep[b][:], scalar=nWin_col[:, och:och + 1],
                        in1=zbf[:], op0=ALU.mult, op1=ALU.add), [ev, repev2, trcf])
                    gz_t = S(lambda e: e.activation(out=zbf[:], in_=zbf[:], func=AF.Silu), [ev])
                    zu = p.dma("gpsimd", "zsp", lambda e, och=och, k=k: e.dma_start(
                        out=z_dram.ap()[och - 8, :, TC * k:TC * (k + 1)], in_=zbf[:]), [gz_t])
                    fs["zup"] = zu
                    ev = gz_t
                fs[("ipfree", par)] = ev
                och_evs.append(ev)

            xdbl_mm = None
            for d in range(8):
                if k == 0:
                    h3 = V(lambda e, d=d, b=b: e.memset(xcb[b][d][:, 0:3], 0.0), [och_evs[d]])
                else:
                    bp = 1 - b
                    h3 = V(lambda e, d=d, b=b, bp=bp: e.tensor_copy(
                        out=xcb[b][d][:, 0:3], in_=xcb[bp][d][:, TC:TC + 3]), [och_evs[d]])
                c1 = V(lambda e, d=d, b=b: e.tensor_scalar(
                    out=xsb[d % 4][:], in0=xcb[b][d][:, 0:TC], scalar1=convw_sb[:, 4 * d:4 * d + 1],
                    scalar2=None, op0=ALU.mult),
                    [h3, vecd] + ([fs[("xsfree", d % 4)]] if ("xsfree", d % 4) in fs else []) +
                    ([fs[("xsmm", d % 4)]] if ("xsmm", d % 4) in fs else []))
                c2 = V(lambda e, d=d, b=b: e.scalar_tensor_tensor(
                    out=xsb[d % 4][:], in0=xcb[b][d][:, 1:TC + 1], scalar=convw_sb[:, 4 * d + 1:4 * d + 2],
                    in1=xsb[d % 4][:], op0=ALU.mult, op1=ALU.add), [c1])
                c3 = V(lambda e, d=d, b=b: e.scalar_tensor_tensor(
                    out=xsb[d % 4][:], in0=xcb[b][d][:, 2:TC + 2], scalar=convw_sb[:, 4 * d + 2:4 * d + 3],
                    in1=xsb[d % 4][:], op0=ALU.mult, op1=ALU.add), [c2])
                c4 = V(lambda e, d=d, b=b: e.scalar_tensor_tensor(
                    out=xsb[d % 4][:], in0=xcb[b][d][:, 3:TC + 3], scalar=convw_sb[:, 4 * d + 3:4 * d + 4],
                    in1=xsb[d % 4][:], op0=ALU.mult, op1=ALU.add), [c3])
                fs[("xcfree", b, d)] = c1
                sl1 = S(lambda e, d=d: e.activation(out=xsb[d % 4][:], in_=xsb[d % 4][:], func=AF.Silu,
                                                    bias=convb_sb[:, d:d + 1]), [c4, vecd])
                xu = p.dma("scalar", f"xssp{d % 4}", lambda e, d=d, k=k: e.dma_start(
                    out=xs_dram.ap()[d, :, TC * k:TC * (k + 1)], in_=xsb[d % 4][:]), [sl1])
                fs[("xsfree", d % 4)] = xu
                xdbl_mm = T(lambda e, d=d: e.matmul(PX[:], xp[d][:], xsb[d % 4][:],
                                                    start=(d == 0), stop=(d == 7)),
                            [sl1, xpdep] + ([xdbl_ev[k - 1]] if (d == 0 and k > 0) else []) +
                            (nwin_evs if (d == 0 and k == 0) else []))
                fs[("xsmm", d % 4)] = xdbl_mm
            xdbl_ev[k] = S(lambda e: e.copy(out=xdblb[:, :], in_=PX[:]),
                           [xdbl_mm] + ([("dma:cc2up", p.dma_cnt.get("cc2up", 0))]
                                        if "cc2up" in p.dma_cnt else []))
            cu = p.dma("sync", "cc2up", lambda e, k=k: e.dma_start(
                out=cc2_in.ap()[k], in_=xdblb[:, :]), [xdbl_ev[k]])
            p.op("gpsimd", lambda e, k=k: e.collective_compute(
                "AllReduce", ALU.add, replica_groups=GRP,
                ins=[cc2_in.ap()[k].opt()], outs=[cc2_out.ap()[k].opt()]), [cu])

        fhead(0)
        fhead(1)
        for k in range(NCH):
            fbody(k)
            if k + 2 < NCH:
                fhead(k + 2)

    p.barrier()

    with contextlib.ExitStack() as stB:
        E = lambda n, s: stB.enter_context(nc.sbuf_tensor(n, s, F32))
        wo = [E(f"wo{b}", [128, 8 * 128]) for b in range(2)]
        dtp = [E(f"dtp{d}", [DTR, 128]) for d in range(8)]
        for d in range(8):
            p.dma("sync", "dtpw", lambda e, d=d: e.dma_start(
                out=dtp[d][:], in_=dtprojT[:, 128 * d:128 * (d + 1)]))
        dtpdep = ("dma:dtpw", p.dma_cnt["dtpw"])
        xdl = E("xdl", [96, TC])
        xs2 = [E(f"xs2_{d}", [128, TC]) for d in range(8)]
        vb = [[E(f"vb{j}_{d}", [128, TC]) for d in range(8)] for j in range(2)]
        z2 = [E(f"z2_{i}", [128, TC]) for i in range(2)]
        dts = [[E(f"dts{j}_{d}", [128, TC]) for d in range(8)] for j in range(2)]
        us = E("us", [128, TC])
        Brep = E("Brep", [128, NST * TC])
        Crep = E("Crep", [128, NST * TC])
        dAb = [E(f"dAb{i}", [128, TC]) for i in range(2)]
        Hb = E("Hb", [128, NST * TC])
        pob = E("pob", [128, TC])
        OPB = [P1, P2, P3, PSs]

        bk = {}
        out3ups = []
        heads = {}
        Hb3 = Hb[:].rearrange("p (a b) -> p a b", b=TC)

        def emit_head(k):
            b = k % 2
            dl = p.dma("sync", "xdl", lambda e, k=k: e.dma_start(
                out=xdl[:], in_=cc2_out.ap()[k]), [bk.get("xdlfree")])
            brl = p.dma("sync", "brepl", lambda e, k=k: e.dma_start(
                out=Brep[:].rearrange("p (a b) -> p a b", b=TC),
                in_=cc2_out.ap()[k][64:80, :].unsqueeze(0).broadcast_to((128, NST, TC))),
                [dl, bk.get("brfree")])
            crl = p.dma("sync", "crepl", lambda e, k=k: e.dma_start(
                out=Crep[:].rearrange("p (a b) -> p a b", b=TC),
                in_=cc2_out.ap()[k][80:96, :].unsqueeze(0).broadcast_to((128, NST, TC))),
                [dl, bk.get("crfree")])
            dt_evs = []
            for d in range(8):
                mm = T(lambda e, d=d: e.matmul(P0[:], dtp[d][:], xdl[0:64, :],
                                               start=True, stop=True),
                       [dl, dtpdep] + ([bk.get("p0free")] if bk.get("p0free") else []))
                sp1 = S(lambda e, d=d, b=b: e.activation(out=dts[b][d][:], in_=P0[:],
                                                         func=AF.Exp, bias=dtb_sb[:, d:d + 1]),
                        [mm, vecd] + ([bk.get(("dtsfree", b))] if bk.get(("dtsfree", b)) else []))
                bk["p0free"] = sp1
                dt_evs.append(S(lambda e, d=d, b=b: e.activation(out=dts[b][d][:], in_=dts[b][d][:],
                                                                 func=AF.Ln, bias=1.0), [sp1]))
            bk["xdlfree"] = dt_evs[-1]
            xl = [p.dma("gpsimd", f"xs2l{d}", lambda e, d=d, k=k: e.dma_start(
                out=xs2[d][:], in_=xs_dram.ap()[d, :, TC * k:TC * (k + 1)]),
                [bk.get(("xsfree", d))]) for d in range(8)]
            heads[k] = (xl, brl, crl, dt_evs)

        def emit_evac(k, och):
            mmtok = bk[("opmm", k, och)]
            pb = OPB[och % 4]
            ev = S(lambda e, pb=pb: e.copy(out=pob[:], in_=pb[:]),
                   [mmtok] + ([bk.get("poup")] if bk.get("poup") else []))
            ou = p.dma("gpsimd", "o3up", lambda e, och=och, k=k: e.dma_start(
                out=cc3_in[k % 4].ap()[(k // 4) * DM + 128 * och:(k // 4) * DM + 128 * (och + 1), :],
                in_=pob[:]), [ev])
            bk["poup"] = ou
            bk[("opev", k, och)] = ev
            bk[("o3", k, och)] = ou
            out3ups.append(ou)

        def emit_tail_mm(k, ochs):
            b = k % 2
            for och in ochs:
                wb = och % 2
                wl = p.dma("sync", f"wol{wb}", lambda e, och=och, wb=wb: e.dma_start(
                    out=wo[wb][:], in_=woutT[och]), [bk.get(("wofree", wb))])
                pb = OPB[och % 4]
                bankdep = bank_token(och % 4)
                mm = None
                for d in range(8):
                    mm = T(lambda e, d=d, wb=wb, pb=pb, b=b: e.matmul(
                        pb[:], wo[wb][:, 128 * d:128 * (d + 1)], vb[b][d][:],
                        start=(d == 0), stop=(d == 7)),
                        [wl, bk[("vbtok", k, d)]] + ([bankdep] if bankdep else []))
                bk[("wofree", wb)] = mm
                bk[("opmm", k, och)] = mm
                bk[("opbank", och % 4)] = ("pending", k, och)

        def bank_token(slot):
            v = bk.get(("opbank", slot))
            if isinstance(v, tuple) and v and v[0] == "pending":
                return bk[("opev", v[1], v[2])]
            return None

        def emit_body(k):
            b = k % 2
            xl, brl, crl, dt_evs = heads[k]
            yout = []
            Hdep = bk.get("Hfree")
            prev_scan_last = bk.get("scanlast")
            for d in range(8):
                zl = p.dma("gpsimd", f"z2l{d % 2}", lambda e, d=d, k=k: e.dma_start(
                    out=z2[d % 2][:], in_=z_dram.ap()[d, :, TC * k:TC * (k + 1)]),
                    [bk.get(("zfree", d % 2))])
                uu = V(lambda e, d=d, b=b: e.tensor_tensor(out=us[:], in0=dts[b][d][:],
                                                           in1=xs2[d][:], op=ALU.mult),
                       [dt_evs[d], xl[d]] + ([bk.get("usfree")] if bk.get("usfree") else []))
                dbx = V(lambda e: e.tensor_tensor(
                    out=Hb3[:, :, :], in0=us[:].unsqueeze(1).broadcast_to((128, NST, TC)),
                    in1=Brep[:].rearrange("p (a b) -> p a b", b=TC), op=ALU.mult),
                    [uu, brl] + ([Hdep] if Hdep else []) +
                    ([bk.get(("hst", d))] if k > 0 else []))
                scan_evs = []
                for n in range(NST):
                    i = n % 2
                    buf_free = ([scan_evs[-2]] if len(scan_evs) >= 2 else
                                ([prev_scan_last] if prev_scan_last else []))
                    da = S(lambda e, d=d, n=n, i=i, b=b: e.activation(
                        out=dAb[i][:], in_=dts[b][d][:], func=AF.Exp,
                        scale=negA_sb[:, NST * d + n:NST * d + n + 1]),
                        [dt_evs[d], negA_t] + buf_free)
                    if k > 0:
                        ini = hstash[:, NST * d + n:NST * d + n + 1]
                    else:
                        ini = 0.0
                    sc = V(lambda e, n=n, i=i, ini=ini: e.tensor_tensor_scan(
                        out=Hb3[:, n, :], data0=dAb[i][:], data1=Hb3[:, n, :],
                        initial=ini, op0=ALU.mult, op1=ALU.add),
                        [da, dbx, hs0])
                    scan_evs.append(sc)
                st_ = S(lambda e, d=d: e.copy(
                    out=hstash[:, NST * d:NST * (d + 1)], in_=Hb3[:, :, TC - 1]), scan_evs[-1:])
                bk[("hst", d)] = st_
                prev_scan_last = scan_evs[-1]
                bk["scanlast"] = prev_scan_last
                hc = V(lambda e: e.tensor_tensor(out=Hb[:], in0=Hb[:], in1=Crep[:], op=ALU.mult),
                       [st_, crl])
                a1_ = V(lambda e: e.tensor_tensor(out=Hb[:, 0:8 * TC], in0=Hb[:, 0:8 * TC],
                                                  in1=Hb[:, 8 * TC:16 * TC], op=ALU.add), [hc])
                a2_ = V(lambda e: e.tensor_tensor(out=Hb[:, 0:4 * TC], in0=Hb[:, 0:4 * TC],
                                                  in1=Hb[:, 4 * TC:8 * TC], op=ALU.add), [a1_])
                a3_ = V(lambda e: e.tensor_tensor(out=Hb[:, 0:2 * TC], in0=Hb[:, 0:2 * TC],
                                                  in1=Hb[:, 2 * TC:4 * TC], op=ALU.add), [a2_])
                yr = V(lambda e: e.tensor_tensor(out=Hb[:, 2 * TC:3 * TC], in0=Hb[:, 0:TC],
                                                 in1=Hb[:, TC:2 * TC], op=ALU.add), [a3_])
                v1 = V(lambda e, d=d: e.scalar_tensor_tensor(
                    out=Hb[:, 2 * TC:3 * TC], in0=xs2[d][:], scalar=dvec_sb[:, d:d + 1],
                    in1=Hb[:, 2 * TC:3 * TC], op0=ALU.mult, op1=ALU.add), [yr, xl[d], vecd])
                bk[("xsfree", d)] = v1
                v2 = V(lambda e, d=d, b=b: e.tensor_tensor(out=vb[b][d][:], in0=Hb[:, 2 * TC:3 * TC],
                                                           in1=z2[d % 2][:], op=ALU.mult),
                       [v1, zl] + ([bk.get(("vbfree", b, d))] if bk.get(("vbfree", b, d)) else []))
                bk[("zfree", d % 2)] = v2
                bk[("vbtok", k, d)] = v2
                Hdep = v2
                yout.append(v2)
                if k > 0 and d < 4:
                    emit_evac(k - 1, d)
                    if d == 3:
                        emit_tail_mm(k - 1, [4, 5, 6, 7])
                        for dd in range(8):
                            bk[("vbfree", (k - 1) % 2, dd)] = bk[("opmm", k - 1, 7)]
                if k > 0 and d >= 4:
                    emit_evac(k - 1, d)
            bk["Hfree"] = Hdep
            bk["usfree"] = yout[-1]
            bk["brfree"] = yout[-1]
            bk["crfree"] = yout[-1]
            bk[("dtsfree", b)] = yout[-1]
            return yout

        rs3 = [None] * 4

        def emit_rs(q):
            deps = [bk[("o3", q, och)] for och in range(8)] + \
                   [bk[("o3", q + 4, och)] for och in range(8)]
            rs3[q] = p.op("gpsimd", lambda e, q=q: e.collective_compute(
                "ReduceScatter", ALU.add, replica_groups=GRP,
                ins=[cc3_in[q].ap().opt()], outs=[cc3_out[q].ap().opt()]), deps)

        emit_head(0)
        for k in range(NCH):
            emit_body(k)
            if k >= 5:
                emit_rs(k - 5)
            if k + 1 < NCH:
                emit_head(k + 1)
            emit_tail_mm(k, [0, 1, 2, 3])
        for och in range(4):
            emit_evac(NCH - 1, och)
        emit_tail_mm(NCH - 1, [4, 5, 6, 7])
        for och in range(4, 8):
            emit_evac(NCH - 1, och)
        emit_rs(3)

    p.barrier()

    with contextlib.ExitStack() as stC:
        X2 = [stC.enter_context(nc.sbuf_tensor(f"X2_{d}", [128, TOK], F32)) for d in range(8)]
        x2d = []
        for d in range(8):
            l1 = p.dma("sync", f"x1re{d}", lambda e, d=d: e.dma_start(
                out=X2[d][:], in_=x1own.ap()[128 * d:128 * (d + 1), :]))
            accs = []
            for t in range(NT):
                accs.append(p.dma("gpsimd", f"prre{d}", lambda e, d=d, t=t: e.dma_start(
                    out=X2[d][:, 512 * t:512 * (t + 1)],
                    in_=cc3_out[t].ap()[128 * d:128 * (d + 1), :],
                    accum_op=ALU.add), [rs3[t], l1] + accs[-1:]))
            x2d.append(accs[-1])
        Wc = load_ffn_weights(stC, "fc")
        Wc["scratch"] = mk_scratch(stC, "fc")
        xotile = stC.enter_context(nc.sbuf_tensor("xotile", [128, 512], F32))
        t2buf = stC.enter_context(nc.sbuf_tensor("t2buf", [128, 512], F32))

        outst = []
        prevC = {}

        def asmC(och, t, pb, mmb, pl, mml, aux):
            par = (t * 8 + och) % 2
            tb = t2buf if par == 0 else aux["sq0"]
            a1 = V(lambda e, pb=pb, t=t, tb=tb: e.scalar_tensor_tensor(
                out=tb[:], in0=pb[:], scalar=0.5,
                in1=Wc["scratch"]["rstd_rep"][:, 512 * t:512 * (t + 1)],
                op0=ALU.mult, op1=ALU.mult),
                [mmb, aux["repd"], aux["sqfree"]] +
                ([prevC.get(("a3", par))] if prevC.get(("a3", par)) else []))
            a1 = V(lambda e, och=och, t=t, tb=tb: e.scalar_tensor_tensor(
                out=tb[:], in0=aux["mr_rep"][:, 512 * t:512 * (t + 1)],
                scalar=aux["nWcols"][:, 10 + och:11 + och], in1=tb[:],
                op0=ALU.mult, op1=ALU.add), [a1, aux["trc"]])
            a2 = V(lambda e, pl=pl, och=och, t=t: e.scalar_tensor_tensor(
                out=xotile[:], in0=pl[:], scalar=0.5, in1=X2[och][:, 512 * t:512 * (t + 1)],
                op0=ALU.mult, op1=ALU.add),
                [mml, x2d[och]] + ([prevC.get("u1")] if prevC.get("u1") else []))
            a3 = V(lambda e, och=och, tb=tb: e.scalar_tensor_tensor(
                out=xotile[:], in0=tb[:], scalar=Wc["halfb"][:, och:och + 1], in1=xotile[:],
                op0=ALU.add, op1=ALU.add), [a1, a2, Wc["hb_t"]])
            u1 = p.dma("sync", "outst", lambda e, och=och, t=t: e.dma_start(
                out=out_t.ap()[128 * och:128 * (och + 1), 512 * t:512 * (t + 1)],
                in_=xotile[:]), [a3])
            outst.append(u1)
            prevC[("a3", par)] = a3
            prevC["u1"] = u1
            return a1, a2

        emit_ffn("fc", X2, x2d, Wc, g2dram, "fc", asmC)
        p.wait("sync", outst)

    p.emit()
    return nc


def kernel(L0_attn_x, L0_attn_in_proj_w, L0_attn_conv_w, L0_attn_conv_b, L0_attn_x_proj_w,
           L0_attn_dt_proj_w, L0_attn_dt_proj_b, L0_attn_A_log, L0_attn_D, L0_attn_out_proj_w,
           L0_ffn_W, L0_ffn_b, L0_ffn_R, L0_ffn_A, L0_ffn_B):
    f32 = np.float32
    asc = np.ascontiguousarray
    x = np.asarray(L0_attn_x, f32)

    ffnWTv = asc(np.asarray(L0_ffn_W, f32).T)
    ffnbv = asc(np.asarray(L0_ffn_b, f32).reshape(8, 128).T)
    ffnRTv = asc(np.asarray(L0_ffn_R, f32).T)
    ffnA2v = asc(np.asarray(L0_ffn_A, f32).transpose(2, 1, 0).reshape(DM, 128))
    ffnB2v = asc(np.asarray(L0_ffn_B, f32).transpose(2, 0, 1).reshape(128, DM))
    identv = np.eye(128, dtype=f32)
    onescv = np.ones((128, 1), f32)
    onesrv = np.ones((1, 128), f32)
    cvalsv2 = np.tile(np.array([[EPS_LN, -0.5]], f32), (128, 1))

    inprj = np.asarray(L0_attn_in_proj_w, f32)
    convwv = np.asarray(L0_attn_conv_w, f32)[:, 0, :]
    convbv = np.asarray(L0_attn_conv_b, f32)
    xprojv = np.asarray(L0_attn_x_proj_w, f32)
    dtprojv = np.asarray(L0_attn_dt_proj_w, f32)
    dtbv = np.asarray(L0_attn_dt_proj_b, f32)
    alogv = np.asarray(L0_attn_A_log, f32)
    dvv = np.asarray(L0_attn_D, f32)
    outprj = np.asarray(L0_attn_out_proj_w, f32)

    in_maps = []
    for c in range(NCORE):
        b, h = c // 2, c % 2
        ch = slice(1024 * h, 1024 * h + 1024)
        zch = slice(2048 + 1024 * h, 2048 + 1024 * h + 1024)
        woT = outprj[:, ch].T
        wo_blocked = asc(woT.reshape(8, 128, 8, 128).transpose(2, 1, 0, 3).reshape(8, 128, 1024))
        in_maps.append({
            "x_in": asc(x[b].T[:, TOK * h:TOK * (h + 1)]),
            "winT": asc(np.concatenate([inprj[ch].T, inprj[zch].T], axis=1)),
            "convw": asc(convwv[ch].reshape(8, 128, 4).transpose(1, 0, 2).reshape(128, 32)),
            "convb": asc(convbv[ch].reshape(8, 128).T),
            "xprojT": asc(xprojv[:, ch].T),
            "dtprojT": asc(dtprojv[ch].T),
            "dtb": asc(dtbv[ch].reshape(8, 128).T),
            "alog": asc(alogv[ch].reshape(8, 128, NST).transpose(1, 0, 2).reshape(128, 8 * NST)),
            "dvec": asc(dvv[ch].reshape(8, 128).T),
            "woutT": wo_blocked,
            "ffnWT": ffnWTv, "ffnb": ffnbv, "ffnRT": ffnRTv, "ffnA2": ffnA2v, "ffnB2": ffnB2v,
            "identv": identv, "onescv": onescv, "onesrv": onesrv, "cvalsv": cvalsv2,
        })

    nc = build_program()
    res = run_bass_kernel_spmd(nc, in_maps, core_ids=list(range(NCORE)))
    out = np.empty((4, L, DM), f32)
    for c in range(NCORE):
        b, h = c // 2, c % 2
        out[b, TOK * h:TOK * (h + 1), :] = np.asarray(res.results[c]["out"]).T
    return out


# version 14 (final)
# speedup vs baseline: 1.0146x; 1.0146x over previous
import contextlib
import os
import numpy as np
import concourse.bass as bass
import concourse.mybir as mybir
from concourse.bass_utils import run_bass_kernel_spmd

F32 = mybir.dt.float32
BF16 = mybir.dt.bfloat16
AF = mybir.ActivationFunctionType
ALU = mybir.AluOpType
AX = mybir.AxisListType

NCORE = 8
DM = 1024
DIN = 1024
L = 4096
TOK = 2048
NST = 16
DTR = 64
TC = 512
NCH = L // TC
NT = TOK // 512
EPS_LN = 1e-5
EPS_GATE = 1e-6
GRP = [[0, 1], [2, 3], [4, 5], [6, 7]]

ENGINES = ("tensor", "vector", "scalar", "gpsimd", "sync")


class Prog:

    def __init__(self, nc):
        self.nc = nc
        self.ops = {e: [] for e in ENGINES}
        self.cnt = {e: 0 for e in ENGINES}
        self.dma_cnt = {}
        self.dma_streams = []

    def op(self, eng, fn, deps=()):
        idx = self.cnt[eng] + 1
        w = self._waits(deps)
        if eng != "tensor" and idx > 1 and w.get(eng, 0) < idx - 1:
            w[eng] = idx - 1
        self.ops[eng].append(("op", fn, w))
        self.cnt[eng] = idx
        return (eng, idx)

    def dma(self, eng, stream, fn, deps=()):
        if stream not in self.dma_cnt:
            self.dma_cnt[stream] = 0
            self.dma_streams.append(stream)
        self.dma_cnt[stream] += 16
        self.ops[eng].append(("dma", fn, self._waits(deps), stream))
        return ("dma:" + stream, self.dma_cnt[stream])

    def wait(self, eng, deps):
        self.ops[eng].append(("wait", self._waits(deps)))

    def _waits(self, deps):
        acc = {}
        for d in deps:
            if d is None:
                continue
            src, val = d
            if val > acc.get(src, 0):
                acc[src] = val
        return acc

    def barrier(self, skip=()):
        toks = [(e, self.cnt[e]) for e in ENGINES if self.cnt[e] > 0 and e not in skip]
        toks += [("dma:" + s, self.dma_cnt[s]) for s in self.dma_streams]
        for e in ENGINES:
            self.wait(e, [t for t in toks if t[0] != e])

    def emit(self):
        nc = self.nc
        self.psem = {e: nc.alloc_semaphore(f"prog_{e}") for e in ENGINES}
        self.dsem = {s: nc.alloc_semaphore(f"dma_{s}") for s in self.dma_streams}

        def sem_of(src):
            return self.dsem[src[4:]] if src.startswith("dma:") else self.psem[src]

        with nc.Block() as block:
            for eng in ENGINES:
                if not self.ops[eng]:
                    continue

                def body(e, eng=eng):
                    for item in self.ops[eng]:
                        kind = item[0]
                        if kind == "wait":
                            for src, val in item[1].items():
                                e.wait_ge(sem_of(src), val)
                            continue
                        fn, waits = item[1], item[2]
                        witems = list(waits.items())
                        for src, val in witems[1:]:
                            e.wait_ge(sem_of(src), val)
                        inst = fn(e)
                        if witems:
                            inst._wait_ge(sem_of(witems[0][0]), witems[0][1])
                        if kind == "dma":
                            inst.then_inc(self.dsem[item[3]], 16)
                        else:
                            inst.then_inc(self.psem[eng], 1)

                getattr(block, eng)(body)


def build_program():
    nc = bass.Bass(trn_type="TRN2", target_bir_lowering=False, num_devices=NCORE)

    DI = lambda n, s: nc.dram_tensor(n, s, F32, kind="ExternalInput").ap()
    x_in = DI("x_in", [DM, TOK])
    winT = DI("winT", [DM, 2 * DIN])
    convw = DI("convw", [128, 32])
    convb = DI("convb", [128, 8])
    xprojT = DI("xprojT", [DIN, 96])
    dtprojT = DI("dtprojT", [DTR, DIN])
    dtb = DI("dtb", [128, 8])
    alog = DI("alog", [128, 8 * NST])
    dvec = DI("dvec", [128, 8])
    woutT = DI("woutT", [8, 128, 8 * 128])
    ffnWT = DI("ffnWT", [DM, DM])
    ffnb = DI("ffnb", [128, 8])
    ffnRT = DI("ffnRT", [DM, 8])
    ffnA2 = DI("ffnA2", [DM, 128])
    ffnB2 = DI("ffnB2", [128, DM])
    identv = DI("identv", [128, 128])
    cvalsv = DI("cvalsv", [128, 2])
    onescv = DI("onescv", [128, 1])
    onesrv = DI("onesrv", [1, 128])

    out_t = nc.dram_tensor("out", [DM, TOK], F32, kind="ExternalOutput")

    cc1_in = nc.dram_tensor("cc1_in", [NT, DM, 512], F32)
    cc1_out = nc.dram_tensor("cc1_out", [NT, 2 * DM, 512], F32)
    x1own = nc.dram_tensor("x1own", [DM, TOK], F32)
    cc2_in = nc.dram_tensor("cc2_in", [NCH, 96, TC], F32)
    cc2_out = nc.dram_tensor("cc2_out", [NCH, 96, TC], F32)
    cc3_in = [nc.dram_tensor(f"cc3i{q}", [2 * DM, 512], F32) for q in range(4)]
    cc3_out = [nc.dram_tensor(f"cc3o{q}", [DM, 512], F32) for q in range(4)]
    xs_dram = nc.dram_tensor("xs_dram", [8, 128, L], F32)
    z_dram = nc.dram_tensor("z_dram", [8, 128, L], F32)
    g1dram = nc.dram_tensor("g1dram", [8, TOK], F32)
    g2dram = nc.dram_tensor("g2dram", [8, TOK], F32)

    p = Prog(nc)
    SB = lambda n, s, dt=F32: nc.alloc_sbuf_tensor(n, s, dt)

    P0 = nc.alloc_psum_tensor("P0", [128, 512], F32)
    P1 = nc.alloc_psum_tensor("P1", [128, 512], F32)
    P2 = nc.alloc_psum_tensor("P2", [128, 512], F32)
    P3 = nc.alloc_psum_tensor("P3", [128, 512], F32)
    PSs = nc.alloc_psum_tensor("PSs", [128, 512], F32)
    PG = nc.alloc_psum_tensor("PG", [8, 512], F32)
    PT = nc.alloc_psum_tensor("PT", [128, 128], F32)
    PX = nc.alloc_psum_tensor("PX", [96, 512], F32)

    T = lambda fn, deps: p.op("tensor", fn, deps)
    V = lambda fn, deps: p.op("vector", fn, deps)
    S = lambda fn, deps: p.op("scalar", fn, deps)

    ident = SB("ident", [128, 128])
    onesc = SB("onesc", [128, 1])
    onesr = SB("onesr", [1, 128])
    p.dma("sync", "cst", lambda e: e.dma_start(out=ident[:], in_=identv))
    p.dma("sync", "cst", lambda e: e.dma_start(out=onesc[:], in_=onescv))
    p.dma("sync", "cst", lambda e: e.dma_start(out=onesr[:], in_=onesrv))
    cvals = SB("cvals", [128, 2])
    p.dma("sync", "cst", lambda e: e.dma_start(out=cvals[:], in_=cvalsv))
    cstd = ("dma:cst", p.dma_cnt["cst"])

    convw_sb = SB("convw_sb", [128, 32])
    convb_sb = SB("convb_sb", [128, 8])
    dtb_sb = SB("dtb_sb", [128, 8])
    dvec_sb = SB("dvec_sb", [128, 8])
    negA_sb = SB("negA_sb", [128, 8 * NST])
    hstash = SB("hstash", [128, 8 * NST])
    p.dma("sync", "vecs", lambda e: e.dma_start(out=convw_sb[:], in_=convw))
    p.dma("sync", "vecs", lambda e: e.dma_start(out=convb_sb[:], in_=convb))
    p.dma("sync", "vecs", lambda e: e.dma_start(out=dtb_sb[:], in_=dtb))
    p.dma("sync", "vecs", lambda e: e.dma_start(out=dvec_sb[:], in_=dvec))
    p.dma("sync", "vecs", lambda e: e.dma_start(out=negA_sb[:], in_=alog))
    vecd = ("dma:vecs", p.dma_cnt["vecs"])
    nA1 = S(lambda e: e.activation(out=negA_sb[:], in_=negA_sb[:], func=AF.Exp), [vecd])
    negA_t = V(lambda e: e.tensor_scalar(out=negA_sb[:], in0=negA_sb[:], scalar1=-1.0,
                                         scalar2=None, op0=ALU.mult), [nA1])
    hs0 = V(lambda e: e.memset(hstash[:], 0.0), [])

    def emit_ffn(tag, X, xdeps, W, gdram, gstream, assemble):
        sc = W["scratch"]
        sq0 = sc["sq0"]
        sq1, acc1, acc2 = sc["sq1"], sc["acc1"], sc["acc2"]
        s1, s2, m_r = sc["s1"], sc["s2"], sc["m"]
        rstd_r = s1
        mr_row = s2
        rstd_rep, mr_rep = sc["rstd_rep"], sc["mr_rep"]
        nWcols = sc["nWcols"]
        nWrow, nRrow, nArow = sc["nWrow"], sc["nRrow"], sc["nArow"]
        glog, GT, grep = sc["glog"], sc["GT"], sc["grep"]
        gfm = glog
        g = sc["gate"]
        h_sb = sc["h_sb"]
        gh = h_sb
        fR, fA = W["RT"], W["A2"]
        wdep = W["wdep"]
        wf = W["wf"]
        Wh, Wl = W["Wh"], W["Wl"]
        Ah, Al, Bh, Bl = W["Ah"], W["Al"], W["Bh"], W["Bl"]
        Xh, Xl = sc["Xh"], sc["Xl"]
        ghh, ghl = sc["ghh"], sc["ghl"]

        rs_mm = [None, None]
        wspl = []
        rs_last = []
        for d in range(8):
            i = d % 2
            wld = p.dma("sync", tag + "wf0", lambda e, d=d, i=i: e.dma_start(
                out=wf[i][:], in_=ffnWT[128 * d:128 * (d + 1), :]),
                (wspl[-1:] + rs_last[-1:]) if wspl else [])
            for c in range(2):
                rs_mm[c] = T(lambda e, d=d, c=c, i=i: e.matmul(
                    PSs[32 * c:32 * c + 1, :], onesc[:], wf[i][:, 512 * c:512 * (c + 1)],
                    start=(d == 0), stop=(d == 7)), [wld, cstd])
            rs_last.append(rs_mm[1])
            sh = S(lambda e, d=d, i=i: e.copy(out=Wh[d][:], in_=wf[i][:]), [wld])
            wspl.append(V(lambda e, d=d, i=i: e.tensor_tensor(
                out=Wl[d][:], in0=wf[i][:], in1=Wh[d][:], op=ALU.subtract), [sh]))
        nwev = [S(lambda e, c=c: e.copy(out=nWrow[0:1, 512 * c:512 * (c + 1)],
                                        in_=PSs[32 * c:32 * c + 1, :]), [rs_mm[c]])
                for c in range(2)]
        nW_t = V(lambda e: e.tensor_scalar(out=nWrow[:], in0=nWrow[:], scalar1=-1.0,
                                           scalar2=None, op0=ALU.mult), nwev)

        ev1s, ev2s = [], []
        mm1d, mm2d = None, None
        for t in range(NT):
            av = V(lambda e, t=t: e.tensor_tensor(
                out=acc1[:], in0=X[0][:, 512 * t:512 * (t + 1)],
                in1=X[1][:, 512 * t:512 * (t + 1)], op=ALU.add),
                [xdeps[0], xdeps[1]] + ([mm1d] if mm1d else []))
            for d in range(2, 8):
                av = V(lambda e, d=d, t=t: e.tensor_tensor(
                    out=acc1[:], in0=acc1[:], in1=X[d][:, 512 * t:512 * (t + 1)],
                    op=ALU.add), [xdeps[d], av])
            mm1d = T(lambda e: e.matmul(PSs[0:1, :], onesc[:], acc1[:],
                                        start=True, stop=True),
                     [av, cstd] + ev1s[-1:] + ev2s[-1:] + (nwev if t == 0 else []))
            ev1s.append(S(lambda e, t=t: e.copy(out=s1[0:1, 512 * t:512 * (t + 1)],
                                                in_=PSs[0:1, :]), [mm1d]))
        sqp = {0: None, 1: None}
        for t in range(NT):
            q0 = S(lambda e, t=t: e.square(out=sq0[:], in_=X[0][:, 512 * t:512 * (t + 1)]),
                   [xdeps[0]] + ([sqp[0]] if sqp[0] else []))
            q1_ = S(lambda e, t=t: e.square(out=sq1[:], in_=X[1][:, 512 * t:512 * (t + 1)]),
                    [xdeps[1]] + ([sqp[1]] if sqp[1] else []))
            av = V(lambda e: e.tensor_tensor(out=acc2[:], in0=sq0[:], in1=sq1[:],
                                             op=ALU.add), [q0, q1_] + ([mm2d] if mm2d else []))
            sqp[0], sqp[1] = av, av
            for d in range(2, 8):
                qd = S(lambda e, d=d, t=t: e.square(
                    out=(sq0 if d % 2 == 0 else sq1)[:], in_=X[d][:, 512 * t:512 * (t + 1)]),
                    [xdeps[d], sqp[d % 2]])
                av = V(lambda e, d=d: e.tensor_tensor(
                    out=acc2[:], in0=acc2[:], in1=(sq0 if d % 2 == 0 else sq1)[:],
                    op=ALU.add), [qd, av])
                sqp[d % 2] = av
            mm2d = T(lambda e: e.matmul(PSs[32:33, :], onesc[:], acc2[:],
                                        start=True, stop=True),
                     [av, cstd, ev1s[-1]] + ev2s[-1:] + (nwev if t == 0 else []))
            ev2s.append(S(lambda e, t=t: e.copy(out=s2[0:1, 512 * t:512 * (t + 1)],
                                                in_=PSs[32:33, :]), [mm2d]))
        r1 = V(lambda e: e.tensor_scalar(out=m_r[:], in0=s1[:], scalar1=1.0 / DM,
                                         scalar2=None, op0=ALU.mult), ev1s)
        r2 = V(lambda e: e.tensor_scalar(out=s2[:], in0=s2[:], scalar1=1.0 / DM,
                                         scalar2=None, op0=ALU.mult), ev2s)
        r3 = V(lambda e: e.tensor_tensor(out=s1[:], in0=m_r[:], in1=m_r[:], op=ALU.mult), [r1])
        r4 = V(lambda e: e.tensor_tensor(out=s2[:], in0=s2[:], in1=s1[:],
                                         op=ALU.subtract), [r2, r3])
        r5 = S(lambda e: e.activation(out=s2[:], in_=s2[:], func=AF.Ln,
                                      bias=cvals[0:1, 0:1]), [r4, cstd])
        r6 = S(lambda e: e.activation(out=rstd_r[:], in_=s2[:], func=AF.Exp,
                                      scale=cvals[0:1, 1:2]), [r5])

        def rowsum(dst, tiles, width, deps, prev):
            evs = list(prev)
            for c in range((width + 511) // 512):
                w = min(512, width - 512 * c)
                mm = None
                for d in range(8):
                    mm = T(lambda e, d=d, c=c, w=w: e.matmul(
                        PSs[64:65, 0:w], onesc[:], tiles[d][:, 512 * c:512 * c + w],
                        start=(d == 0), stop=(d == 7)), deps + [cstd] + evs[-1:])
                evs.append(S(lambda e, c=c, w=w: e.copy(
                    out=dst[0:1, 512 * c:512 * c + w], in_=PSs[64:65, 0:w]), [mm]))
            neg = V(lambda e: e.tensor_scalar(out=dst[:], in0=dst[:], scalar1=-1.0,
                                              scalar2=None, op0=ALU.mult), evs[-1:])
            return neg, evs[-1]

        nR_t, e_b = rowsum(nRrow, fR, 8, [wdep], ev2s[-1:] + ev1s[-1:])
        nA_t, e_c = rowsum(nArow, fA, 128, [wdep], [e_b])

        trs_c = []
        for c8 in range(8):
            trs_c.append(T(lambda e, c8=c8: e.matmul(
                PT[:, c8:c8 + 1], nWrow[0:1, 128 * c8:128 * (c8 + 1)], ident[0:1, 0:1],
                is_transpose=True, start=True, stop=True, skip_group_check=True),
                [nW_t, cstd]))
        trs_c.append(T(lambda e: e.matmul(
            PT[0:8, 8:9], nRrow[0:1, 0:8], ident[0:1, 0:1],
            is_transpose=True, start=True, stop=True, skip_group_check=True), [nR_t, cstd]))
        trs_c.append(T(lambda e: e.matmul(
            PT[:, 9:10], nArow[0:1, :], ident[0:1, 0:1],
            is_transpose=True, start=True, stop=True, skip_group_check=True), [nA_t, cstd]))
        S(lambda e: e.copy(out=nWcols[:, 0:8], in_=PT[:, 0:8]), trs_c)
        S(lambda e: e.copy(out=nWcols[0:8, 8:9], in_=PT[0:8, 8:9]), [])
        trc0 = S(lambda e: e.copy(out=nWcols[:, 9:10], in_=PT[:, 9:10]), [])
        trc = V(lambda e: e.tensor_scalar(out=nWcols[:, 10:18], in0=nWcols[:, 0:8],
                                          scalar1=0.5, scalar2=None, op0=ALU.mult), [trc0])

        mr_v = V(lambda e: e.tensor_tensor(out=mr_row[:], in0=m_r[:], in1=rstd_r[:],
                                           op=ALU.mult), [r6, r1])
        repev = []
        for t in range(NT):
            mmb = T(lambda e, t=t: e.matmul(PSs[:], onesr[:], rstd_r[0:1, 512 * t:512 * (t + 1)],
                                            start=True, stop=True),
                    [r6, cstd, e_c] + repev[-1:])
            repev.append(S(lambda e, t=t: e.copy(out=rstd_rep[:, 512 * t:512 * (t + 1)],
                                                 in_=PSs[:]), [mmb]))
            mmb2 = T(lambda e, t=t: e.matmul(PSs[:], onesr[:], mr_row[0:1, 512 * t:512 * (t + 1)],
                                             start=True, stop=True),
                     [mr_v, cstd] + repev[-1:])
            repev.append(S(lambda e, t=t: e.copy(out=mr_rep[:, 512 * t:512 * (t + 1)],
                                                 in_=PSs[:]), [mmb2]))
        repd = repev[-1]

        lg_ev = []
        for t in range(NT):
            mm = None
            for d in range(8):
                mm = T(lambda e, d=d, t=t: e.matmul(
                    PG[0:8, :], fR[d][:], X[d][:, 512 * t:512 * (t + 1)],
                    start=(d == 0), stop=(d == 7)), [xdeps[d], wdep] + lg_ev[-1:])
            lgm = V(lambda e, t=t: e.tensor_tensor(
                out=glog[:, 512 * t:512 * (t + 1)], in0=PG[0:8, :],
                in1=rstd_rep[0:8, 512 * t:512 * (t + 1)], op=ALU.mult), [mm, repd])
            lg_ev.append(V(lambda e, t=t: e.scalar_tensor_tensor(
                out=glog[:, 512 * t:512 * (t + 1)], in0=mr_rep[0:8, 512 * t:512 * (t + 1)],
                scalar=nWcols[0:8, 8:9], in1=glog[:, 512 * t:512 * (t + 1)],
                op0=ALU.mult, op1=ALU.add), [lgm, trc]))

        trs = []
        for j in range(16):
            trs.append(T(lambda e, j=j: e.matmul(
                PT[:, 8 * j:8 * (j + 1)], glog[0:8, 128 * j:128 * (j + 1)], ident[0:8, 0:8],
                is_transpose=True, start=True, stop=True, skip_group_check=True),
                [lg_ev[j // 4], cstd, trc0]))
        gt_ev = S(lambda e: e.copy(out=GT[:], in_=PT[:]), trs + lg_ev)

        GT3 = lambda t_: t_[:].rearrange("p (a b) -> p a b", b=8)
        g1 = V(lambda e: e.reduce_max(out=g["mx"][:], in_=GT3(GT), axis=AX.X), [gt_ev])
        g2 = V(lambda e: e.tensor_tensor(out=GT3(g["t0"]), in0=GT3(GT),
                                         in1=g["mx"][:].unsqueeze(2).broadcast_to((128, 16, 8)),
                                         op=ALU.subtract), [g1])
        g3 = S(lambda e: e.activation(out=g["t1"][:], in_=g["t0"][:], func=AF.Exp), [g2])
        g4 = V(lambda e: e.reduce_sum(out=g["sm"][:], in_=GT3(g["t1"]), axis=AX.X), [g3])
        g5 = V(lambda e: e.reciprocal(out=g["rc"][:], in_=g["sm"][:]), [g4])
        g6 = V(lambda e: e.tensor_tensor(out=GT3(g["t0"]), in0=GT3(g["t1"]),
                                         in1=g["rc"][:].unsqueeze(2).broadcast_to((128, 16, 8)),
                                         op=ALU.mult), [g5, g3])
        topd = []
        for j in range(16):
            topd.append(V(lambda e, j=j: e.max(out=g["tp"][:, 8 * j:8 * (j + 1)],
                                               in_=g["t0"][:, 8 * j:8 * (j + 1)]), [g6]))
        g7 = V(lambda e: e.tensor_tensor(
            out=GT3(g["t1"]), in0=GT3(g["t0"]),
            in1=g["tp"][:].rearrange("p (a b) -> p a b", b=8)[:, :, 1:2].broadcast_to((128, 16, 8)),
            op=ALU.is_ge), topd)
        g8 = V(lambda e: e.tensor_tensor(out=g["t1"][:], in0=g["t0"][:], in1=g["t1"][:],
                                         op=ALU.mult), [g7])
        g9 = V(lambda e: e.reduce_sum(out=g["sm"][:], in_=GT3(g["t1"]), axis=AX.X), [g8])
        g10 = V(lambda e: e.tensor_scalar(out=g["sm"][:], in0=g["sm"][:], scalar1=EPS_GATE,
                                          scalar2=None, op0=ALU.add), [g9])
        g11 = V(lambda e: e.reciprocal(out=g["rc"][:], in_=g["sm"][:]), [g10])
        g12 = V(lambda e: e.tensor_tensor(out=GT3(g["t0"]), in0=GT3(g["t1"]),
                                          in1=g["rc"][:].unsqueeze(2).broadcast_to((128, 16, 8)),
                                          op=ALU.mult), [g11, g8])

        gfm_ev = []
        for t in range(NT):
            trb = []
            for jj in range(4):
                j = 4 * t + jj
                trb.append(T(lambda e, j=j, jj=jj: e.matmul(
                    PG[0:8, 128 * jj:128 * (jj + 1)], g["t0"][:, 8 * j:8 * (j + 1)], ident[:],
                    is_transpose=True, start=True, stop=True, skip_group_check=True),
                    [g12, cstd] + gfm_ev[-1:] + lg_ev[-1:]))
            gfm_ev.append(S(lambda e, t=t: e.copy(out=gfm[0:8, 512 * t:512 * (t + 1)],
                                                  in_=PG[0:8, :]), trb))
        up = p.dma("gpsimd", tag + "gup", lambda e: e.dma_start(out=gdram.ap(), in_=gfm[0:8, :]),
                   gfm_ev)
        repds = []
        for r in range(16):
            repds.append(p.dma("gpsimd", tag + "grep", lambda e, r=r: e.dma_start(
                out=grep[8 * r:8 * (r + 1), :], in_=gdram.ap()), [up]))

        freeB = {0: None, 1: None}
        freeL = None
        tok = {}
        for t in range(NT):
            xspl = []
            for d in range(8):
                shx = S(lambda e, d=d, t=t: e.copy(out=Xh[d][:], in_=X[d][:, 512 * t:512 * (t + 1)]),
                        [xdeps[d]] + ([tok[(t - 1, "xfree")]] if (t - 1, "xfree") in tok else []))
                xspl.append(V(lambda e, d=d, t=t: e.tensor_tensor(
                    out=Xl[d][:], in0=X[d][:, 512 * t:512 * (t + 1)], in1=Xh[d][:],
                    op=ALU.subtract), [shx]))
            mm = None
            for d in range(8):
                deph = [W["adep"], xspl[d]] + ([tok[(t - 1, "hfree")]] if (d == 0 and (t - 1, "hfree") in tok) else [])
                mm = T(lambda e, d=d: e.matmul(P3[:], Ah[d][:], Xh[d][:],
                                               start=(d == 0), stop=False), deph)
                mm = T(lambda e, d=d: e.matmul(P3[:], Ah[d][:], Xl[d][:],
                                               start=False, stop=False), [])
                mm = T(lambda e, d=d: e.matmul(P3[:], Al[d][:], Xh[d][:],
                                               start=False, stop=(d == 7)), [])
            hv0 = V(lambda e, t=t: e.tensor_tensor(
                out=h_sb[:, 512 * t:512 * (t + 1)], in0=P3[:],
                in1=rstd_rep[:, 512 * t:512 * (t + 1)], op=ALU.mult), [mm, repd])
            hv = V(lambda e, t=t: e.scalar_tensor_tensor(
                out=h_sb[:, 512 * t:512 * (t + 1)], in0=mr_rep[:, 512 * t:512 * (t + 1)],
                scalar=nWcols[:, 9:10], in1=h_sb[:, 512 * t:512 * (t + 1)],
                op0=ALU.mult, op1=ALU.add), [hv0, trc])
            tok[(t, "hfree")] = hv
            ghv = V(lambda e, t=t: e.tensor_tensor(
                out=gh[:, 512 * t:512 * (t + 1)], in0=h_sb[:, 512 * t:512 * (t + 1)],
                in1=grep[:, 512 * t:512 * (t + 1)], op=ALU.mult), [hv] + repds)
            shg = S(lambda e, t=t: e.copy(out=ghh[:], in_=gh[:, 512 * t:512 * (t + 1)]),
                    [ghv] + ([tok[(t - 1, "gfree")]] if (t - 1, "gfree") in tok else []))
            svg = V(lambda e, t=t: e.tensor_tensor(
                out=ghl[:], in0=gh[:, 512 * t:512 * (t + 1)], in1=ghh[:],
                op=ALU.subtract), [shg])
            for och in range(8):
                par = (t * 8 + och) % 2
                pb = P0 if par == 0 else P1
                mmb = None
                for d in range(8):
                    dep0 = []
                    if och == 0:
                        dep0 = [wspl[d], xspl[d]]
                    if d == 0 and freeB[par]:
                        dep0 = dep0 + [freeB[par]]
                    mmb = T(lambda e, d=d, och=och, pb=pb: e.matmul(
                        pb[:], Wh[d][:, 128 * och:128 * (och + 1)], Xh[d][:],
                        start=(d == 0), stop=False), dep0)
                    mmb = T(lambda e, d=d, och=och, pb=pb: e.matmul(
                        pb[:], Wh[d][:, 128 * och:128 * (och + 1)], Xl[d][:],
                        start=False, stop=False), [])
                    mmb = T(lambda e, d=d, och=och, pb=pb: e.matmul(
                        pb[:], Wl[d][:, 128 * och:128 * (och + 1)], Xh[d][:],
                        start=False, stop=(d == 7)), [])
                mml = T(lambda e, och=och: e.matmul(
                    P2[:], Bh[:, 128 * och:128 * (och + 1)], ghh[:],
                    start=True, stop=False), [W["bdep2"], svg] + ([freeL] if freeL else []))
                mml = T(lambda e, och=och: e.matmul(
                    P2[:], Bh[:, 128 * och:128 * (och + 1)], ghl[:],
                    start=False, stop=False), [])
                mml = T(lambda e, och=och: e.matmul(
                    P2[:], Bl[:, 128 * och:128 * (och + 1)], ghh[:],
                    start=False, stop=True), [])
                fb_, fl_ = assemble(och, t, pb, mmb, P2, mml,
                                    {"rstd_rep": rstd_rep, "mr_rep": mr_rep,
                                     "nWcols": nWcols, "trc": trc, "repd": repd,
                                     "sq0": sq0, "sqfree": ev2s[-1]})
                freeB[par] = fb_
                freeL = fl_
            tok[(t, "xfree")] = freeB[(t * 8 + 7) % 2]
            tok[(t, "gfree")] = freeL

    def load_ffn_weights(stack, tag):
        SBx = lambda n, s, dt=F32: stack.enter_context(nc.sbuf_tensor(tag + n, s, dt))
        fR = [SBx(f"R{d}", [128, 8]) for d in range(8)]
        fA = [SBx(f"A{d}", [128, 128]) for d in range(8)]
        fB2 = SBx("B2", [128, DM])
        fb = SBx("b", [128, 8])
        wf = [SBx("wf0", [128, DM])] * 2
        Wh = [SBx(f"Wh{d}", [128, DM], BF16) for d in range(8)]
        Wl = [SBx(f"Wl{d}", [128, DM], BF16) for d in range(8)]
        Ah = [SBx(f"Ah{d}", [128, 128], BF16) for d in range(8)]
        Al = [SBx(f"Al{d}", [128, 128], BF16) for d in range(8)]
        Bh = SBx("Bh", [128, DM], BF16)
        Bl = SBx("Bl", [128, DM], BF16)
        st = tag + "w"
        for d in range(8):
            p.dma("sync", st, lambda e, d=d, fR=fR: e.dma_start(out=fR[d][:], in_=ffnRT[128 * d:128 * (d + 1), :]))
            p.dma("sync", st, lambda e, d=d, fA=fA: e.dma_start(out=fA[d][:], in_=ffnA2[128 * d:128 * (d + 1), :]))
        p.dma("sync", st, lambda e, fB2=fB2: e.dma_start(out=fB2[:], in_=ffnB2))
        p.dma("sync", st, lambda e, fb=fb: e.dma_start(out=fb[:], in_=ffnb))
        wdep = ("dma:" + st, p.dma_cnt[st])
        aspl = []
        for d in range(8):
            sh = S(lambda e, d=d, fA=fA, Ah=Ah: e.copy(out=Ah[d][:], in_=fA[d][:]), [wdep])
            aspl.append(V(lambda e, d=d, fA=fA, Ah=Ah, Al=Al: e.tensor_tensor(
                out=Al[d][:], in0=fA[d][:], in1=Ah[d][:], op=ALU.subtract), [sh]))
        shb = S(lambda e, fB2=fB2, Bh=Bh: e.copy(out=Bh[:], in_=fB2[:]), [wdep])
        svb = V(lambda e, fB2=fB2, Bh=Bh, Bl=Bl: e.tensor_tensor(
            out=Bl[:], in0=fB2[:], in1=Bh[:], op=ALU.subtract), [shb])
        halfb = SBx("hb", [128, 8])
        hb_t = V(lambda e, halfb=halfb, fb=fb: e.tensor_scalar(
            out=halfb[:], in0=fb[:], scalar1=0.5, scalar2=None, op0=ALU.mult), [wdep])
        return {"RT": fR, "A2": fA, "wdep": wdep, "halfb": halfb, "hb_t": hb_t,
                "wf": wf, "Wh": Wh, "Wl": Wl, "Ah": Ah, "Al": Al, "Bh": Bh, "Bl": Bl,
                "adep": aspl[-1], "bdep2": svb}

    def mk_scratch(stack, tag):
        E = lambda n, s: stack.enter_context(nc.sbuf_tensor(tag + n, s, F32))
        Eb = lambda n, s: stack.enter_context(nc.sbuf_tensor(tag + n, s, BF16))
        h_sb = E("hsb", [128, TOK])
        mr_rep = E("mrrep", [128, TOK])
        return {
            "sq0": E("sq0", [128, 512]), "sq1": h_sb[:, 1024:1536],
            "acc1": h_sb[:, 0:512], "acc2": h_sb[:, 512:1024],
            "s1": E("s1", [1, TOK]), "s2": E("s2", [1, TOK]), "m": mr_rep[0:1, :],
            "rstd_rep": E("rrep", [128, TOK]), "mr_rep": mr_rep,
            "nWcols": E("nWcols", [128, 18]),
            "nWrow": E("nW", [1, DM]), "nRrow": E("nR", [1, 8]), "nArow": E("nA", [1, 128]),
            "glog": E("glog", [8, TOK]), "GT": E("GT", [128, 128]),
            "gate": {"mx": E("gmx", [128, 16]), "sm": E("gsm", [128, 16]),
                     "rc": E("grc", [128, 16]), "tp": E("gtp", [128, 128]),
                     "t0": E("gt0", [128, 128]), "t1": E("gt1", [128, 128])},
            "grep": E("grep2", [128, TOK]),
            "h_sb": h_sb,
            "Xh": [Eb(f"Xh{d}", [128, 512]) for d in range(8)],
            "Xl": [Eb(f"Xl{d}", [128, 512]) for d in range(8)],
            "ghh": Eb("ghh", [128, 512]),
            "ghl": Eb("ghl", [128, 512]),
        }

    with contextlib.ExitStack() as stA:
        X0 = [stA.enter_context(nc.sbuf_tensor(f"X0_{d}", [128, TOK], F32)) for d in range(8)]
        x0d = [p.dma("sync", f"x0_{d}", lambda e, d=d: e.dma_start(
            out=X0[d][:], in_=x_in[128 * d:128 * (d + 1), :])) for d in range(8)]
        Wd = load_ffn_weights(stA, "fa")
        Wd["scratch"] = mk_scratch(stA, "fa")
        x1tile = stA.enter_context(nc.sbuf_tensor("x1tile", [128, 512], F32))
        t1buf = stA.enter_context(nc.sbuf_tensor("t1buf", [128, 512], F32))

        up1 = []
        upt = []
        ag4 = [None] * NT
        prevA = {}

        def asmA(och, t, pb, mmb, pl, mml, aux):
            par = (t * 8 + och) % 2
            tb = t1buf if par == 0 else aux["sq0"]
            a1 = V(lambda e, pb=pb, t=t, tb=tb: e.scalar_tensor_tensor(
                out=tb[:], in0=pb[:], scalar=0.5,
                in1=Wd["scratch"]["rstd_rep"][:, 512 * t:512 * (t + 1)],
                op0=ALU.mult, op1=ALU.mult),
                [mmb, aux["repd"], aux["sqfree"]] +
                ([prevA.get(("a3", par))] if prevA.get(("a3", par)) else []))
            a1 = V(lambda e, och=och, t=t, tb=tb: e.scalar_tensor_tensor(
                out=tb[:], in0=aux["mr_rep"][:, 512 * t:512 * (t + 1)],
                scalar=aux["nWcols"][:, 10 + och:11 + och], in1=tb[:],
                op0=ALU.mult, op1=ALU.add), [a1, aux["trc"]])
            a2 = V(lambda e, pl=pl, och=och, t=t: e.scalar_tensor_tensor(
                out=x1tile[:], in0=pl[:], scalar=0.5, in1=X0[och][:, 512 * t:512 * (t + 1)],
                op0=ALU.mult, op1=ALU.add),
                [mml] + ([prevA.get("u1"), prevA.get("u2")] if prevA.get("u1") else []))
            a3 = V(lambda e, och=och, tb=tb: e.scalar_tensor_tensor(
                out=x1tile[:], in0=tb[:], scalar=Wd["halfb"][:, och:och + 1], in1=x1tile[:],
                op0=ALU.add, op1=ALU.add), [a1, a2, Wd["hb_t"]])
            u1 = p.dma("sync", "x1up", lambda e, och=och, t=t: e.dma_start(
                out=cc1_in.ap()[t][128 * och:128 * (och + 1), :],
                in_=x1tile[:]), [a3])
            u2 = p.dma("scalar", "x1up2", lambda e, och=och, t=t: e.dma_start(
                out=x1own.ap()[128 * och:128 * (och + 1), 512 * t:512 * (t + 1)],
                in_=x1tile[:]), [a3])
            up1.extend([u1, u2])
            upt.append(u1)
            if och == 7:
                ag4[t] = p.op("gpsimd", lambda e, t=t: e.collective_compute(
                    "AllGather", ALU.bypass, replica_groups=GRP,
                    ins=[cc1_in.ap()[t].opt()], outs=[cc1_out.ap()[t].opt()]), list(upt))
                del upt[:]
            prevA[("a3", par)] = a3
            prevA["u1"], prevA["u2"] = u1, u2
            return a1, a2

        emit_ffn("fa", X0, x0d, Wd, g1dram, "fa", asmA)
        if os.environ.get("KBISECT") == "A":
            st = p.dma("sync", "outst", lambda e: e.dma_start(out=out_t.ap()[0:128, 0:512],
                                                              in_=x1tile[:]), [prevA["a3"]])
            p.wait("sync", up1 + [st])
            p.emit()
            return nc
    p.barrier(skip=("gpsimd",))

    with contextlib.ExitStack() as stF:
        E = lambda n, s: stF.enter_context(nc.sbuf_tensor(n, s, F32))
        wf = [E(f"wf{i}", [128, 2 * DIN]) for i in range(2)]
        winh = [stF.enter_context(nc.sbuf_tensor(f"winh{d}", [128, 2 * DIN], BF16)) for d in range(8)]
        winl = [stF.enter_context(nc.sbuf_tensor(f"winl{d}", [128, 2 * DIN], BF16)) for d in range(8)]
        xp = [E(f"xp{d}", [128, 96]) for d in range(8)]
        for d in range(8):
            p.dma("sync", "xpw", lambda e, d=d: e.dma_start(
                out=xp[d][:], in_=xprojT[128 * d:128 * (d + 1), :]))
        xpdep = ("dma:xpw", p.dma_cnt["xpw"])

        nWin = E("nWin", [1, 2 * DIN])
        nWin_col = E("nWin_col", [128, 16])
        x1c = [E(f"x1c_{d}", [128, TC]) for d in range(8)]
        x1h = [[stF.enter_context(nc.sbuf_tensor(f"x1h{b}_{d}", [128, TC], BF16)) for d in range(8)] for b in range(2)]
        x1l = [[stF.enter_context(nc.sbuf_tensor(f"x1l{b}_{d}", [128, TC], BF16)) for d in range(8)] for b in range(2)]
        sqf = E("sqf", [128, TC])
        accf1 = wf[0][:, 0:512]
        accf2 = wf[0][:, 512:1024]
        sqf1 = wf[0][:, 1024:1536]
        mr2r = wf[1][0:1, 0:1024]
        mr2rep = [wf[1][:, 1024:1536], wf[1][:, 1536:2048]]
        xdblb = E("xdblb", [96, TC])
        m2r = E("m2r", [1, 2 * TC])
        v2r = E("v2r", [1, 2 * TC])
        rstd2r = E("rstd2r", [1, 2 * TC])
        r2rep = [E(f"r2rep{b}", [128, TC]) for b in range(2)]
        xcb = [[E(f"xcb{b}_{d}", [128, TC + 3]) for d in range(8)] for b in range(2)]
        xsb = [E(f"xsb{d}", [128, TC]) for d in range(4)]
        zbf = E("zbf", [128, TC])

        rs_mm = [None] * 4
        spl_evs = []
        rs_last = []
        for d in range(8):
            i = d % 2
            wld = p.dma("sync", f"winw{i}", lambda e, d=d, i=i: e.dma_start(
                out=wf[i][:], in_=winT[128 * d:128 * (d + 1), :]),
                (spl_evs[-2:-1] + rs_last[-2:-1]) if len(spl_evs) >= 2 else [])
            for c in range(4):
                pr = PSs[32 * c:32 * c + 1, :] if c < 2 else PX[32 * (c - 2):32 * (c - 2) + 1, :]
                rs_mm[c] = T(lambda e, d=d, c=c, i=i, pr=pr: e.matmul(
                    pr, onesc[:], wf[i][:, 512 * c:512 * (c + 1)],
                    start=(d == 0), stop=(d == 7)), [wld, cstd])
            rs_last.append(rs_mm[3])
            sh = S(lambda e, d=d, i=i: e.copy(out=winh[d][:], in_=wf[i][:]), [wld])
            spl_evs.append(V(lambda e, d=d, i=i: e.tensor_tensor(
                out=winl[d][:], in0=wf[i][:], in1=winh[d][:], op=ALU.subtract), [sh]))
        nwin_evs = [S(lambda e, c=c: e.copy(
            out=nWin[0:1, 512 * c:512 * (c + 1)],
            in_=(PSs[32 * c:32 * c + 1, :] if c < 2 else PX[32 * (c - 2):32 * (c - 2) + 1, :])),
            [rs_mm[c]]) for c in range(4)]
        nWin_t = V(lambda e: e.tensor_scalar(out=nWin[:], in0=nWin[:], scalar1=-1.0,
                                             scalar2=None, op0=ALU.mult), nwin_evs)
        trs_f = []
        for c8 in range(16):
            trs_f.append(T(lambda e, c8=c8: e.matmul(
                PT[:, c8:c8 + 1], nWin[0:1, 128 * c8:128 * (c8 + 1)], ident[0:1, 0:1],
                is_transpose=True, start=True, stop=True, skip_group_check=True),
                [nWin_t, cstd]))
        trcf = S(lambda e: e.copy(out=nWin_col[:], in_=PT[:, 0:16]), trs_f)
        winspl = {d: spl_evs[d] for d in range(8)}

        fs = {"st1": nwin_evs[-1], "st2": nwin_evs[-1], "reps": nwin_evs[-1]}
        xdbl_ev = [None] * NCH

        def fhead(k):
            b = k % 2
            ldk = []
            for d in range(8):
                dep = [ag4[k % 4]] + ([fs[("x1free", d)]] if ("x1free", d) in fs else []) + \
                    ([fs["acc1"]] if "acc1" in fs else []) + \
                    ([fs["acc2"]] if "acc2" in fs else [])
                ldk.append(p.dma("sync", f"x1c_{d}", lambda e, d=d, k=k: e.dma_start(
                    out=x1c[d][:],
                    in_=cc1_out.ap()[k % 4][(k // 4) * DM + 128 * d:(k // 4) * DM + 128 * (d + 1),
                                            :]), dep))
            av = V(lambda e: e.tensor_tensor(out=accf1[:], in0=x1c[0][:], in1=x1c[1][:],
                                             op=ALU.add),
                   [ldk[0], ldk[1]] + ([fs["acc1"]] if "acc1" in fs else [rs_last[6]]))
            for d in range(2, 8):
                av = V(lambda e, d=d: e.tensor_tensor(out=accf1[:], in0=accf1[:],
                                                      in1=x1c[d][:], op=ALU.add), [ldk[d], av])
            mm1 = T(lambda e: e.matmul(PSs[0:1, :], onesc[:], accf1[:], start=True, stop=True),
                    [av, cstd, fs["reps"]] + ([fs["st1"]] if "st1" in fs else []))
            fs["acc1"] = mm1
            e1 = S(lambda e, k=k: e.copy(out=m2r[0:1, TC * (k % 2):TC * (k % 2 + 1)], in_=PSs[0:1, :]), [mm1])
            fs["st1"] = e1
            q0s = S(lambda e: e.square(out=sqf[:], in_=x1c[0][:]),
                    [ldk[0]] + ([fs["sqp0"]] if "sqp0" in fs else []))
            q1s = S(lambda e: e.square(out=sqf1[:], in_=x1c[1][:]),
                    [ldk[1]] + ([fs["sqp1"]] if "sqp1" in fs else [rs_last[6]]))
            av = V(lambda e: e.tensor_tensor(out=accf2[:], in0=sqf[:], in1=sqf1[:],
                                             op=ALU.add),
                   [q0s, q1s] + ([fs["acc2"]] if "acc2" in fs else []))
            fs["sqp0"], fs["sqp1"] = av, av
            for d in range(2, 8):
                qd = S(lambda e, d=d: e.square(out=(sqf if d % 2 == 0 else sqf1)[:],
                                               in_=x1c[d][:]),
                       [ldk[d], fs["sqp0" if d % 2 == 0 else "sqp1"]])
                av = V(lambda e, d=d: e.tensor_tensor(
                    out=accf2[:], in0=accf2[:], in1=(sqf if d % 2 == 0 else sqf1)[:],
                    op=ALU.add), [qd, av])
                fs["sqp0" if d % 2 == 0 else "sqp1"] = av
            mm2 = T(lambda e: e.matmul(PSs[32:33, :], onesc[:], accf2[:], start=True, stop=True),
                    [av, cstd, fs["reps"]] + ([fs["st2"]] if "st2" in fs else []))
            fs["acc2"] = mm2
            e2 = S(lambda e, k=k: e.copy(out=v2r[0:1, TC * (k % 2):TC * (k % 2 + 1)], in_=PSs[32:33, :]), [mm2])
            fs["st2"] = e2
            xspl = []
            for d in range(8):
                sh = S(lambda e, d=d, b=b: e.copy(out=x1h[b][d][:], in_=x1c[d][:]),
                       [ldk[d]] + ([fs[("xspl", b, d)]] if ("xspl", b, d) in fs else []))
                sv = V(lambda e, d=d, b=b: e.tensor_tensor(
                    out=x1l[b][d][:], in0=x1c[d][:], in1=x1h[b][d][:], op=ALU.subtract), [sh])
                fs[("xspl", b, d)] = sv
                fs[("x1free", d)] = sv
                xspl.append(sv)
            kb = k % 2
            q1 = V(lambda e, kb=kb: e.tensor_scalar(
                out=m2r[0:1, TC * kb:TC * (kb + 1)], in0=m2r[0:1, TC * kb:TC * (kb + 1)],
                scalar1=1.0 / DM, scalar2=None, op0=ALU.mult), [e1])
            q2 = V(lambda e, kb=kb: e.tensor_scalar(
                out=v2r[0:1, TC * kb:TC * (kb + 1)], in0=v2r[0:1, TC * kb:TC * (kb + 1)],
                scalar1=1.0 / DM, scalar2=None, op0=ALU.mult), [e2])
            q3 = V(lambda e, kb=kb: e.tensor_tensor(
                out=rstd2r[0:1, TC * kb:TC * (kb + 1)], in0=m2r[0:1, TC * kb:TC * (kb + 1)],
                in1=m2r[0:1, TC * kb:TC * (kb + 1)], op=ALU.mult), [q1])
            q4 = V(lambda e, kb=kb: e.tensor_tensor(
                out=v2r[0:1, TC * kb:TC * (kb + 1)], in0=v2r[0:1, TC * kb:TC * (kb + 1)],
                in1=rstd2r[0:1, TC * kb:TC * (kb + 1)], op=ALU.subtract), [q2, q3])
            q5 = S(lambda e, kb=kb: e.activation(out=rstd2r[0:1, TC * kb:TC * (kb + 1)],
                                               in_=v2r[0:1, TC * kb:TC * (kb + 1)],
                                               func=AF.Ln, bias=cvals[0:1, 0:1]), [q4, cstd])
            q6 = S(lambda e, kb=kb: e.activation(out=rstd2r[0:1, TC * kb:TC * (kb + 1)],
                                               in_=rstd2r[0:1, TC * kb:TC * (kb + 1)],
                                               func=AF.Exp, scale=cvals[0:1, 1:2]), [q5])
            mmb = T(lambda e, kb=kb: e.matmul(PSs[:], onesr[:], rstd2r[0:1, TC * kb:TC * (kb + 1)],
                                            start=True, stop=True),
                    [q6, cstd] + ([fs["reps"]] if "reps" in fs else []))
            repev = S(lambda e, b=b: e.copy(out=r2rep[b][:], in_=PSs[:]), [mmb])
            mr2v = V(lambda e, kb=kb: e.tensor_tensor(
                out=mr2r[0:1, TC * kb:TC * (kb + 1)], in0=m2r[0:1, TC * kb:TC * (kb + 1)],
                in1=rstd2r[0:1, TC * kb:TC * (kb + 1)], op=ALU.mult),
                [q6, q1] + ([rs_last[7]] if k == 0 else []))
            mmb2 = T(lambda e, kb=kb: e.matmul(PSs[:], onesr[:], mr2r[0:1, TC * kb:TC * (kb + 1)],
                                               start=True, stop=True), [mr2v, cstd, repev])
            repev2 = S(lambda e, b=b: e.copy(out=mr2rep[b][:], in_=PSs[:]), [mmb2])
            fs["reps"] = repev2
            fs[("hk", k)] = (xspl, repev, repev2)

        def fbody(k):
            b = k % 2
            xspl, repev, repev2 = fs[("hk", k)]
            och_evs = []
            for och in range(16):
                par = och % 2
                pb = P0 if par == 0 else P1
                mm = None
                for d in range(8):
                    dep0 = ([winspl[d], xspl[d]] +
                            ([fs[("ipfree", par)]] if ("ipfree", par) in fs else [])) \
                        if d == 0 else [winspl[d], xspl[d]]
                    mm = T(lambda e, d=d, och=och, b=b, pb=pb: e.matmul(
                        pb[:], winh[d][:, 128 * och:128 * (och + 1)], x1h[b][d][:],
                        start=(d == 0), stop=False), dep0)
                    mm = T(lambda e, d=d, och=och, b=b, pb=pb: e.matmul(
                        pb[:], winh[d][:, 128 * och:128 * (och + 1)], x1l[b][d][:],
                        start=False, stop=False), [])
                    mm = T(lambda e, d=d, och=och, b=b, pb=pb: e.matmul(
                        pb[:], winl[d][:, 128 * och:128 * (och + 1)], x1h[b][d][:],
                        start=False, stop=(d == 7)), [])
                if och < 8:
                    ev = V(lambda e, och=och, b=b, pb=pb: e.tensor_tensor(
                        out=xcb[b][och][:, 3:TC + 3], in0=pb[:], in1=r2rep[b][:], op=ALU.mult),
                        [mm, repev] + ([fs[("xcfree", b, och)]] if ("xcfree", b, och) in fs else []))
                    ev = V(lambda e, och=och, b=b: e.scalar_tensor_tensor(
                        out=xcb[b][och][:, 3:TC + 3], in0=mr2rep[b][:],
                        scalar=nWin_col[:, och:och + 1], in1=xcb[b][och][:, 3:TC + 3],
                        op0=ALU.mult, op1=ALU.add), [ev, repev2, trcf])
                else:
                    ev = V(lambda e, b=b, pb=pb: e.tensor_tensor(
                        out=zbf[:], in0=pb[:], in1=r2rep[b][:], op=ALU.mult),
                        [mm, repev] + ([fs["zup"]] if "zup" in fs else []))
                    ev = V(lambda e, och=och, b=b: e.scalar_tensor_tensor(
                        out=zbf[:], in0=mr2rep[b][:], scalar=nWin_col[:, och:och + 1],
                        in1=zbf[:], op0=ALU.mult, op1=ALU.add), [ev, repev2, trcf])
                    gz_t = S(lambda e: e.activation(out=zbf[:], in_=zbf[:], func=AF.Silu), [ev])
                    zu = p.dma("gpsimd", "zsp", lambda e, och=och, k=k: e.dma_start(
                        out=z_dram.ap()[och - 8, :, TC * k:TC * (k + 1)], in_=zbf[:]), [gz_t])
                    fs["zup"] = zu
                    ev = gz_t
                fs[("ipfree", par)] = ev
                och_evs.append(ev)

            xdbl_mm = None
            for d in range(8):
                if k == 0:
                    h3 = V(lambda e, d=d, b=b: e.memset(xcb[b][d][:, 0:3], 0.0), [och_evs[d]])
                else:
                    bp = 1 - b
                    h3 = V(lambda e, d=d, b=b, bp=bp: e.tensor_copy(
                        out=xcb[b][d][:, 0:3], in_=xcb[bp][d][:, TC:TC + 3]), [och_evs[d]])
                c1 = V(lambda e, d=d, b=b: e.tensor_scalar(
                    out=xsb[d % 4][:], in0=xcb[b][d][:, 0:TC], scalar1=convw_sb[:, 4 * d:4 * d + 1],
                    scalar2=None, op0=ALU.mult),
                    [h3, vecd] + ([fs[("xsfree", d % 4)]] if ("xsfree", d % 4) in fs else []) +
                    ([fs[("xsmm", d % 4)]] if ("xsmm", d % 4) in fs else []))
                c2 = V(lambda e, d=d, b=b: e.scalar_tensor_tensor(
                    out=xsb[d % 4][:], in0=xcb[b][d][:, 1:TC + 1], scalar=convw_sb[:, 4 * d + 1:4 * d + 2],
                    in1=xsb[d % 4][:], op0=ALU.mult, op1=ALU.add), [c1])
                c3 = V(lambda e, d=d, b=b: e.scalar_tensor_tensor(
                    out=xsb[d % 4][:], in0=xcb[b][d][:, 2:TC + 2], scalar=convw_sb[:, 4 * d + 2:4 * d + 3],
                    in1=xsb[d % 4][:], op0=ALU.mult, op1=ALU.add), [c2])
                c4 = V(lambda e, d=d, b=b: e.scalar_tensor_tensor(
                    out=xsb[d % 4][:], in0=xcb[b][d][:, 3:TC + 3], scalar=convw_sb[:, 4 * d + 3:4 * d + 4],
                    in1=xsb[d % 4][:], op0=ALU.mult, op1=ALU.add), [c3])
                fs[("xcfree", b, d)] = c1
                sl1 = S(lambda e, d=d: e.activation(out=xsb[d % 4][:], in_=xsb[d % 4][:], func=AF.Silu,
                                                    bias=convb_sb[:, d:d + 1]), [c4, vecd])
                xu = p.dma("scalar", f"xssp{d % 4}", lambda e, d=d, k=k: e.dma_start(
                    out=xs_dram.ap()[d, :, TC * k:TC * (k + 1)], in_=xsb[d % 4][:]), [sl1])
                fs[("xsfree", d % 4)] = xu
                xdbl_mm = T(lambda e, d=d: e.matmul(PX[:], xp[d][:], xsb[d % 4][:],
                                                    start=(d == 0), stop=(d == 7)),
                            [sl1, xpdep] + ([xdbl_ev[k - 1]] if (d == 0 and k > 0) else []) +
                            (nwin_evs if (d == 0 and k == 0) else []))
                fs[("xsmm", d % 4)] = xdbl_mm
            xdbl_ev[k] = S(lambda e: e.copy(out=xdblb[:, :], in_=PX[:]),
                           [xdbl_mm] + ([("dma:cc2up", p.dma_cnt.get("cc2up", 0))]
                                        if "cc2up" in p.dma_cnt else []))
            cu = p.dma("sync", "cc2up", lambda e, k=k: e.dma_start(
                out=cc2_in.ap()[k], in_=xdblb[:, :]), [xdbl_ev[k]])
            p.op("gpsimd", lambda e, k=k: e.collective_compute(
                "AllReduce", ALU.add, replica_groups=GRP,
                ins=[cc2_in.ap()[k].opt()], outs=[cc2_out.ap()[k].opt()]), [cu])

        fhead(0)
        fhead(1)
        for k in range(NCH):
            fbody(k)
            if k + 2 < NCH:
                fhead(k + 2)

    p.barrier()

    with contextlib.ExitStack() as stB:
        E = lambda n, s: stB.enter_context(nc.sbuf_tensor(n, s, F32))
        wo = [E(f"wo{b}", [128, 8 * 128]) for b in range(2)]
        dtp = [E(f"dtp{d}", [DTR, 128]) for d in range(8)]
        for d in range(8):
            p.dma("sync", "dtpw", lambda e, d=d: e.dma_start(
                out=dtp[d][:], in_=dtprojT[:, 128 * d:128 * (d + 1)]))
        dtpdep = ("dma:dtpw", p.dma_cnt["dtpw"])
        xdl = E("xdl", [96, TC])
        xs2one = [E(f"xs2_{d}", [128, TC]) for d in range(8)]
        xs2 = [xs2one, xs2one]
        vb = [[E(f"vb{j}_{d}", [128, TC]) for d in range(8)] for j in range(2)]
        z2 = [E(f"z2_{i}", [128, TC]) for i in range(2)]
        dts = [[E(f"dts{j}_{d}", [128, TC]) for d in range(8)] for j in range(2)]
        us = E("us", [128, TC])
        Brep = E("Brep", [128, NST * TC])
        Crep = E("Crep", [128, NST * TC])
        dAb = [E(f"dAb{i}", [128, TC]) for i in range(2)]
        Hb = E("Hb", [128, NST * TC])
        pob = E("pob", [128, TC])
        OPB = [P1, P2, P3, PSs]

        bk = {}
        out3ups = []
        heads = {}
        Hb3 = Hb[:].rearrange("p (a b) -> p a b", b=TC)

        def emit_head(k):
            b = k % 2
            dl = p.dma("sync", "xdl", lambda e, k=k: e.dma_start(
                out=xdl[:], in_=cc2_out.ap()[k]), [bk.get("xdlfree")])
            brl = p.dma("sync", "brepl", lambda e, k=k: e.dma_start(
                out=Brep[:].rearrange("p (a b) -> p a b", b=TC),
                in_=cc2_out.ap()[k][64:80, :].unsqueeze(0).broadcast_to((128, NST, TC))),
                [dl, bk.get("brfree")])
            crl = p.dma("sync", "crepl", lambda e, k=k: e.dma_start(
                out=Crep[:].rearrange("p (a b) -> p a b", b=TC),
                in_=cc2_out.ap()[k][80:96, :].unsqueeze(0).broadcast_to((128, NST, TC))),
                [dl, bk.get("crfree")])
            dt_evs = []
            for d in range(8):
                mm = T(lambda e, d=d: e.matmul(P0[:], dtp[d][:], xdl[0:64, :],
                                               start=True, stop=True),
                       [dl, dtpdep] + ([bk.get("p0free")] if bk.get("p0free") else []))
                sp1 = S(lambda e, d=d, b=b: e.activation(out=dts[b][d][:], in_=P0[:],
                                                         func=AF.Exp, bias=dtb_sb[:, d:d + 1]),
                        [mm, vecd] + ([bk.get(("dtsfree", b))] if bk.get(("dtsfree", b)) else []))
                bk["p0free"] = sp1
                dt_evs.append(S(lambda e, d=d, b=b: e.activation(out=dts[b][d][:], in_=dts[b][d][:],
                                                                 func=AF.Ln, bias=1.0), [sp1]))
            bk["xdlfree"] = dt_evs[-1]
            xl = [p.dma("gpsimd", f"xs2l{d}", lambda e, d=d, k=k, b=b: e.dma_start(
                out=xs2[b][d][:], in_=xs_dram.ap()[d, :, TC * k:TC * (k + 1)]),
                [bk.get(("xsfree", d))]) for d in range(8)]
            heads[k] = (xl, brl, crl, dt_evs)

        def emit_evac(k, och):
            mmtok = bk[("opmm", k, och)]
            pb = OPB[och % 4]
            ev = S(lambda e, pb=pb: e.copy(out=pob[:], in_=pb[:]),
                   [mmtok] + ([bk.get("poup")] if bk.get("poup") else []))
            ou = p.dma("scalar", "o3up", lambda e, och=och, k=k: e.dma_start(
                out=cc3_in[k % 4].ap()[(k // 4) * DM + 128 * och:(k // 4) * DM + 128 * (och + 1), :],
                in_=pob[:]), [ev])
            bk["poup"] = ou
            bk[("opev", k, och)] = ev
            bk[("o3", k, och)] = ou
            out3ups.append(ou)

        def emit_tail_mm(k, ochs):
            b = k % 2
            for och in ochs:
                wb = och % 2
                wl = p.dma("sync", f"wol{wb}", lambda e, och=och, wb=wb: e.dma_start(
                    out=wo[wb][:], in_=woutT[och]), [bk.get(("wofree", wb))])
                pb = OPB[och % 4]
                bankdep = bank_token(och % 4)
                mm = None
                for d in range(8):
                    mm = T(lambda e, d=d, wb=wb, pb=pb, b=b: e.matmul(
                        pb[:], wo[wb][:, 128 * d:128 * (d + 1)], vb[b][d][:],
                        start=(d == 0), stop=(d == 7)),
                        [wl, bk[("vbtok", k, d)]] + ([bankdep] if bankdep else []))
                bk[("wofree", wb)] = mm
                bk[("opmm", k, och)] = mm
                bk[("opbank", och % 4)] = ("pending", k, och)

        def bank_token(slot):
            v = bk.get(("opbank", slot))
            if isinstance(v, tuple) and v and v[0] == "pending":
                return bk[("opev", v[1], v[2])]
            return None

        def emit_body(k):
            b = k % 2
            xl, brl, crl, dt_evs = heads[k]
            yout = []
            Hdep = bk.get("Hfree")
            prev_scan_last = bk.get("scanlast")
            for d in range(8):
                zl = p.dma("gpsimd", f"z2l{d % 2}", lambda e, d=d, k=k: e.dma_start(
                    out=z2[d % 2][:], in_=z_dram.ap()[d, :, TC * k:TC * (k + 1)]),
                    [bk.get(("zfree", d % 2))])
                uu = V(lambda e, d=d, b=b: e.tensor_tensor(out=us[:], in0=dts[b][d][:],
                                                           in1=xs2[b][d][:], op=ALU.mult),
                       [dt_evs[d], xl[d]] + ([bk.get("usfree")] if bk.get("usfree") else []))
                dbx = V(lambda e: e.tensor_tensor(
                    out=Hb3[:, :, :], in0=us[:].unsqueeze(1).broadcast_to((128, NST, TC)),
                    in1=Brep[:].rearrange("p (a b) -> p a b", b=TC), op=ALU.mult),
                    [uu, brl] + ([Hdep] if Hdep else []) +
                    ([bk.get(("hst", d))] if k > 0 else []))
                scan_evs = []
                for n in range(NST):
                    i = n % 2
                    buf_free = ([scan_evs[-2]] if len(scan_evs) >= 2 else
                                ([prev_scan_last] if prev_scan_last else []))
                    da = S(lambda e, d=d, n=n, i=i, b=b: e.activation(
                        out=dAb[i][:], in_=dts[b][d][:], func=AF.Exp,
                        scale=negA_sb[:, NST * d + n:NST * d + n + 1]),
                        [dt_evs[d], negA_t] + buf_free)
                    if k > 0:
                        ini = hstash[:, NST * d + n:NST * d + n + 1]
                    else:
                        ini = 0.0
                    sc = V(lambda e, n=n, i=i, ini=ini: e.tensor_tensor_scan(
                        out=Hb3[:, n, :], data0=dAb[i][:], data1=Hb3[:, n, :],
                        initial=ini, op0=ALU.mult, op1=ALU.add),
                        [da, dbx, hs0])
                    scan_evs.append(sc)
                st_ = S(lambda e, d=d: e.copy(
                    out=hstash[:, NST * d:NST * (d + 1)], in_=Hb3[:, :, TC - 1]), scan_evs[-1:])
                bk[("hst", d)] = st_
                prev_scan_last = scan_evs[-1]
                bk["scanlast"] = prev_scan_last
                hc = V(lambda e: e.tensor_tensor(out=Hb[:], in0=Hb[:], in1=Crep[:], op=ALU.mult),
                       [st_, crl])
                a1_ = V(lambda e: e.tensor_tensor(out=Hb[:, 0:8 * TC], in0=Hb[:, 0:8 * TC],
                                                  in1=Hb[:, 8 * TC:16 * TC], op=ALU.add), [hc])
                a2_ = V(lambda e: e.tensor_tensor(out=Hb[:, 0:4 * TC], in0=Hb[:, 0:4 * TC],
                                                  in1=Hb[:, 4 * TC:8 * TC], op=ALU.add), [a1_])
                a3_ = V(lambda e: e.tensor_tensor(out=Hb[:, 0:2 * TC], in0=Hb[:, 0:2 * TC],
                                                  in1=Hb[:, 2 * TC:4 * TC], op=ALU.add), [a2_])
                yr = V(lambda e: e.tensor_tensor(out=Hb[:, 2 * TC:3 * TC], in0=Hb[:, 0:TC],
                                                 in1=Hb[:, TC:2 * TC], op=ALU.add), [a3_])
                v1 = V(lambda e, d=d, b=b: e.scalar_tensor_tensor(
                    out=Hb[:, 2 * TC:3 * TC], in0=xs2[b][d][:], scalar=dvec_sb[:, d:d + 1],
                    in1=Hb[:, 2 * TC:3 * TC], op0=ALU.mult, op1=ALU.add), [yr, xl[d], vecd])
                bk[("xsfree", d)] = v1
                v2 = V(lambda e, d=d, b=b: e.tensor_tensor(out=vb[b][d][:], in0=Hb[:, 2 * TC:3 * TC],
                                                           in1=z2[d % 2][:], op=ALU.mult),
                       [v1, zl] + ([bk.get(("vbfree", b, d))] if bk.get(("vbfree", b, d)) else []))
                bk[("zfree", d % 2)] = v2
                bk[("vbtok", k, d)] = v2
                Hdep = v2
                yout.append(v2)
                if k > 0 and d < 4:
                    emit_evac(k - 1, d)
                    if d == 3:
                        emit_tail_mm(k - 1, [4, 5, 6, 7])
                        for dd in range(8):
                            bk[("vbfree", (k - 1) % 2, dd)] = bk[("opmm", k - 1, 7)]
                if k > 0 and d >= 4:
                    emit_evac(k - 1, d)
            bk["Hfree"] = Hdep
            bk["usfree"] = yout[-1]
            bk["brfree"] = yout[-1]
            bk["crfree"] = yout[-1]
            bk[("dtsfree", b)] = yout[-1]
            return yout

        rs3 = [None] * 4

        def emit_rs(q):
            deps = [bk[("o3", q, och)] for och in range(8)] + \
                   [bk[("o3", q + 4, och)] for och in range(8)]
            rs3[q] = p.op("gpsimd", lambda e, q=q: e.collective_compute(
                "ReduceScatter", ALU.add, replica_groups=GRP,
                ins=[cc3_in[q].ap().opt()], outs=[cc3_out[q].ap().opt()]), deps)

        emit_head(0)
        for k in range(NCH):
            emit_body(k)
            if k >= 5:
                emit_rs(k - 5)
            if k + 1 < NCH:
                emit_head(k + 1)
            emit_tail_mm(k, [0, 1, 2, 3])
        for och in range(4):
            emit_evac(NCH - 1, och)
        emit_tail_mm(NCH - 1, [4, 5, 6, 7])
        for och in range(4, 8):
            emit_evac(NCH - 1, och)
        emit_rs(3)

    p.barrier()

    with contextlib.ExitStack() as stC:
        X2 = [stC.enter_context(nc.sbuf_tensor(f"X2_{d}", [128, TOK], F32)) for d in range(8)]
        x2d = []
        for d in range(8):
            l1 = p.dma("sync", f"x1re{d}", lambda e, d=d: e.dma_start(
                out=X2[d][:], in_=x1own.ap()[128 * d:128 * (d + 1), :]))
            accs = []
            for t in range(NT):
                accs.append(p.dma("gpsimd", f"prre{d}", lambda e, d=d, t=t: e.dma_start(
                    out=X2[d][:, 512 * t:512 * (t + 1)],
                    in_=cc3_out[t].ap()[128 * d:128 * (d + 1), :],
                    accum_op=ALU.add), [rs3[t], l1] + accs[-1:]))
            x2d.append(accs[-1])
        Wc = load_ffn_weights(stC, "fc")
        Wc["scratch"] = mk_scratch(stC, "fc")
        xotile = stC.enter_context(nc.sbuf_tensor("xotile", [128, 512], F32))
        t2buf = stC.enter_context(nc.sbuf_tensor("t2buf", [128, 512], F32))

        outst = []
        prevC = {}

        def asmC(och, t, pb, mmb, pl, mml, aux):
            par = (t * 8 + och) % 2
            tb = t2buf if par == 0 else aux["sq0"]
            a1 = V(lambda e, pb=pb, t=t, tb=tb: e.scalar_tensor_tensor(
                out=tb[:], in0=pb[:], scalar=0.5,
                in1=Wc["scratch"]["rstd_rep"][:, 512 * t:512 * (t + 1)],
                op0=ALU.mult, op1=ALU.mult),
                [mmb, aux["repd"], aux["sqfree"]] +
                ([prevC.get(("a3", par))] if prevC.get(("a3", par)) else []))
            a1 = V(lambda e, och=och, t=t, tb=tb: e.scalar_tensor_tensor(
                out=tb[:], in0=aux["mr_rep"][:, 512 * t:512 * (t + 1)],
                scalar=aux["nWcols"][:, 10 + och:11 + och], in1=tb[:],
                op0=ALU.mult, op1=ALU.add), [a1, aux["trc"]])
            a2 = V(lambda e, pl=pl, och=och, t=t: e.scalar_tensor_tensor(
                out=xotile[:], in0=pl[:], scalar=0.5, in1=X2[och][:, 512 * t:512 * (t + 1)],
                op0=ALU.mult, op1=ALU.add),
                [mml, x2d[och]] + ([prevC.get("u1")] if prevC.get("u1") else []))
            a3 = V(lambda e, och=och, tb=tb: e.scalar_tensor_tensor(
                out=xotile[:], in0=tb[:], scalar=Wc["halfb"][:, och:och + 1], in1=xotile[:],
                op0=ALU.add, op1=ALU.add), [a1, a2, Wc["hb_t"]])
            u1 = p.dma("sync", "outst", lambda e, och=och, t=t: e.dma_start(
                out=out_t.ap()[128 * och:128 * (och + 1), 512 * t:512 * (t + 1)],
                in_=xotile[:]), [a3])
            outst.append(u1)
            prevC[("a3", par)] = a3
            prevC["u1"] = u1
            return a1, a2

        emit_ffn("fc", X2, x2d, Wc, g2dram, "fc", asmC)
        p.wait("sync", outst)

    p.emit()
    return nc


def kernel(L0_attn_x, L0_attn_in_proj_w, L0_attn_conv_w, L0_attn_conv_b, L0_attn_x_proj_w,
           L0_attn_dt_proj_w, L0_attn_dt_proj_b, L0_attn_A_log, L0_attn_D, L0_attn_out_proj_w,
           L0_ffn_W, L0_ffn_b, L0_ffn_R, L0_ffn_A, L0_ffn_B):
    f32 = np.float32
    asc = np.ascontiguousarray
    x = np.asarray(L0_attn_x, f32)

    ffnWTv = asc(np.asarray(L0_ffn_W, f32).T)
    ffnbv = asc(np.asarray(L0_ffn_b, f32).reshape(8, 128).T)
    ffnRTv = asc(np.asarray(L0_ffn_R, f32).T)
    ffnA2v = asc(np.asarray(L0_ffn_A, f32).transpose(2, 1, 0).reshape(DM, 128))
    ffnB2v = asc(np.asarray(L0_ffn_B, f32).transpose(2, 0, 1).reshape(128, DM))
    identv = np.eye(128, dtype=f32)
    onescv = np.ones((128, 1), f32)
    onesrv = np.ones((1, 128), f32)
    cvalsv2 = np.tile(np.array([[EPS_LN, -0.5]], f32), (128, 1))

    inprj = np.asarray(L0_attn_in_proj_w, f32)
    convwv = np.asarray(L0_attn_conv_w, f32)[:, 0, :]
    convbv = np.asarray(L0_attn_conv_b, f32)
    xprojv = np.asarray(L0_attn_x_proj_w, f32)
    dtprojv = np.asarray(L0_attn_dt_proj_w, f32)
    dtbv = np.asarray(L0_attn_dt_proj_b, f32)
    alogv = np.asarray(L0_attn_A_log, f32)
    dvv = np.asarray(L0_attn_D, f32)
    outprj = np.asarray(L0_attn_out_proj_w, f32)

    in_maps = []
    for c in range(NCORE):
        b, h = c // 2, c % 2
        ch = slice(1024 * h, 1024 * h + 1024)
        zch = slice(2048 + 1024 * h, 2048 + 1024 * h + 1024)
        woT = outprj[:, ch].T
        wo_blocked = asc(woT.reshape(8, 128, 8, 128).transpose(2, 1, 0, 3).reshape(8, 128, 1024))
        in_maps.append({
            "x_in": asc(x[b].T[:, TOK * h:TOK * (h + 1)]),
            "winT": asc(np.concatenate([inprj[ch].T, inprj[zch].T], axis=1)),
            "convw": asc(convwv[ch].reshape(8, 128, 4).transpose(1, 0, 2).reshape(128, 32)),
            "convb": asc(convbv[ch].reshape(8, 128).T),
            "xprojT": asc(xprojv[:, ch].T),
            "dtprojT": asc(dtprojv[ch].T),
            "dtb": asc(dtbv[ch].reshape(8, 128).T),
            "alog": asc(alogv[ch].reshape(8, 128, NST).transpose(1, 0, 2).reshape(128, 8 * NST)),
            "dvec": asc(dvv[ch].reshape(8, 128).T),
            "woutT": wo_blocked,
            "ffnWT": ffnWTv, "ffnb": ffnbv, "ffnRT": ffnRTv, "ffnA2": ffnA2v, "ffnB2": ffnB2v,
            "identv": identv, "onescv": onescv, "onesrv": onesrv, "cvalsv": cvalsv2,
        })

    nc = build_program()
    res = run_bass_kernel_spmd(nc, in_maps, core_ids=list(range(NCORE)))
    out = np.empty((4, L, DM), f32)
    for c in range(NCORE):
        b, h = c // 2, c % 2
        out[b, TOK * h:TOK * (h + 1), :] = np.asarray(res.results[c]["out"]).T
    return out
